# version 06
# baseline (speedup 1.0000x reference)
import numpy as np
import ml_dtypes
import concourse.bass as bass
import concourse.mybir as mybir
from contextlib import ExitStack
from concourse.bass_utils import run_bass_kernel_spmd

F32 = mybir.dt.float32
BF16 = mybir.dt.bfloat16
I16 = mybir.dt.int16
I32 = mybir.dt.int32
U32 = mybir.dt.uint32
AF = mybir.ActivationFunctionType
ALU = mybir.AluOpType
AX = mybir.AxisListType

ENGS = ("tensor", "vector", "scalar", "gpsimd", "sync")

T = 4096
D = 1024
CAP = 2048
NE = 16
EPS_LN = 1e-5
ALPHA = 0.9
SCALE = 0.125

class Tok:
    __slots__ = ("sem", "val", "stream")

    def __init__(s, sem, val, stream=None):
        s.sem, s.val, s.stream = sem, val, stream


class Buf:
    __slots__ = ("w", "readers", "name")

    def __init__(s, name=""):
        s.w, s.readers, s.name = None, [], name


class K:
    def __init__(self, nc):
        self.nc = nc
        self.prog = {e: [] for e in ENGS}
        self.esem = {e: nc.alloc_semaphore(f"es_{e}") for e in ENGS}
        self.ecnt = {e: 0 for e in ENGS}
        self.water = {e: {} for e in ENGS}
        self.dsem = {}
        self.dcnt = {}
        self.marks = [("init", {e: 0 for e in ENGS})]

    def mark(self, name):
        self.marks.append((name, {e: len(self.prog[e]) for e in ENGS}))

    def fence(self):
        toks = [Tok(self.esem[e], self.ecnt[e]) for e in ENGS if self.ecnt[e] > 0]
        toks += [Tok(self.dsem[s], self.dcnt[s]) for s in self.dsem
                 if self.dcnt[s] > 0]
        self.fence_toks = toks

    def _stream(self, name):
        if name not in self.dsem:
            self.dsem[name] = self.nc.alloc_semaphore(f"ds_{name}")
            self.dcnt[name] = 0
        return self.dsem[name]

    def _mkwaits(self, e, deps):
        w = self.water[e]
        need = {}
        for t in deps:
            if t is None:
                continue
            kk = id(t.sem)
            val = self.dcnt[t.stream] if t.stream is not None else t.val
            if w.get(kk, 0) >= val:
                continue
            if need.get(kk, (None, 0))[1] < val:
                need[kk] = (t.sem, val)
        for kk, (sem, v) in need.items():
            w[kk] = v
        return list(need.values())

    def _deps(self, reads, writes):
        deps = []
        for b in reads:
            deps.append(b.w)
        for b in writes:
            deps.append(b.w)
            deps.extend(b.readers)
        return deps

    def _track(self, reads, writes, tok):
        for b in reads:
            b.readers.append(tok)
        for b in writes:
            b.w = tok
            b.readers = []

    def op(self, e, emit, reads=(), writes=(), extra=()):
        deps = self._deps(reads, writes) + list(extra) + \
            getattr(self, "fence_toks", [])
        wl = self._mkwaits(e, deps)
        self.ecnt[e] += 1
        tok = Tok(self.esem[e], self.ecnt[e])

        def clo(eng, wl=wl, emit=emit, tok=tok):
            for (sem, v) in wl:
                eng.wait_ge(sem, v)
            emit(eng).then_inc(tok.sem, 1)

        self.prog[e].append(clo)
        self._track(reads, writes, tok)
        return tok

    def dma(self, e, dst, src, reads=(), writes=(), stream="d0", extra=(), **kw):
        deps = self._deps(reads, writes) + list(extra) + \
            getattr(self, "fence_toks", [])
        wl = self._mkwaits(e, deps)
        sem = self._stream(stream)
        self.dcnt[stream] += 16
        tok = Tok(sem, self.dcnt[stream], stream)

        def clo(eng, wl=wl, dst=dst, src=src, tok=tok, kw=kw):
            for (s, v) in wl:
                eng.wait_ge(s, v)
            eng.dma_start(dst, src, **kw).then_inc(tok.sem, 16)

        self.prog[e].append(clo)
        self._track(reads, writes, tok)
        return tok

    def dop(self, e, emit, reads=(), writes=(), stream="g0", extra=()):
        deps = self._deps(reads, writes) + list(extra) + \
            getattr(self, "fence_toks", [])
        wl = self._mkwaits(e, deps)
        sem = self._stream(stream)
        self.dcnt[stream] += 16
        tok = Tok(sem, self.dcnt[stream], stream)

        def clo(eng, wl=wl, emit=emit, tok=tok):
            for (s, v) in wl:
                eng.wait_ge(s, v)
            emit(eng).then_inc(tok.sem, 16)

        self.prog[e].append(clo)
        self._track(reads, writes, tok)
        return tok

    def finalize(self, final_waits):
        nc = self.nc
        wl = self._mkwaits("sync", list(final_waits))

        def tail(eng, wl=wl):
            for (sem, v) in wl:
                eng.wait_ge(sem, v)

        self.prog["sync"].append(tail)
        marks = self.marks

        def run_eng(eng, e):
            bounds = [m[1][e] for m in marks] + [len(self.prog[e])]
            for i, (name, _) in enumerate(marks):
                a, b = bounds[i], bounds[i + 1]
                if a >= b:
                    continue
                with nc.named_scope(name):
                    for f in self.prog[e][a:b]:
                        f(eng)

        with nc.Block() as blk:
            @blk.tensor
            def _(t):
                run_eng(t, "tensor")

            @blk.vector
            def _(v):
                run_eng(v, "vector")

            @blk.scalar
            def _(s):
                run_eng(s, "scalar")

            @blk.gpsimd
            def _(g):
                run_eng(g, "gpsimd")

            @blk.sync
            def _(s):
                run_eng(s, "sync")


def emit_gj(k, flat_t, flat_b, n, width, scr_t, scr_b, gjmask_t, gjmask_b):
    rcol = scr_t[:, 0:1]
    rowsc = scr_t[:, 1:1 + n]
    colf = scr_t[:, 1 + n:1 + 2 * n]
    upd = scr_t[:, 1 + 2 * n:1 + 2 * n + n * n]
    P = 128
    for w in range(width):
        blk = flat_t[:, w]
        blk3 = flat_t[:, w].rearrange("p (r c) -> p r c", r=n)
        for j in range(n):
            piv = blk[:, j * n + j: j * n + j + 1]
            k.op("vector", lambda v, piv=piv: v.reciprocal(rcol, piv),
                 reads=[flat_b], writes=[scr_b])
            rowj = blk[:, j * n: j * n + n]
            k.op("vector", lambda v, rowj=rowj:
                 v.tensor_scalar(rowsc, rowj, rcol, None, op0=ALU.mult),
                 reads=[flat_b, scr_b], writes=[scr_b])
            colj = blk3[:, :, j]
            k.op("vector", lambda v, colj=colj, j=j:
                 v.scalar_tensor_tensor(colf, colj, -1.0, gjmask_t[:, j],
                                        op0=ALU.mult, op1=ALU.mult),
                 reads=[flat_b, gjmask_b], writes=[scr_b])
            k.op("vector", lambda v:
                 v.tensor_mul(upd.rearrange("p (r c) -> p r c", r=n),
                              colf[:, :, None].broadcast_to([P, n, n]),
                              rowsc[:, None, :].broadcast_to([P, n, n])),
                 reads=[scr_b], writes=[scr_b])
            k.op("vector", lambda v, blk=blk: v.tensor_add(blk, blk, upd),
                 reads=[scr_b], writes=[flat_b])
            k.op("vector", lambda v, rowj=rowj: v.tensor_copy(rowj, rowsc),
                 reads=[scr_b], writes=[flat_b])
            k.op("vector", lambda v, colj=colj:
                 v.tensor_scalar(colj, colf, rcol, None, op0=ALU.mult),
                 reads=[scr_b], writes=[flat_b])
            k.op("vector", lambda v, piv=piv: v.tensor_copy(piv, rcol),
                 reads=[scr_b], writes=[flat_b])



def emit_border_inv(k, G_t, G_b, inv_t, inv_b, t0_t, t0_b, tt_t, tt_b,
                    stg_t, stg_b, is16_t, is16_b, q2_t, q2_b,
                    flat_t, flat_b, flat2_t, flat2_b, scr_t, scr_b,
                    gjm_t, gjm_b, identcol_t, identcol_b,
                    bounce, bounce_b, ps, psb, npair=64):
    NP = npair
    P_ = 128

    def extract(src_fn, read_bufs):
        for h in range(2):
            k.dma("sync", bounce.rearrange("(u h) (r c) -> h r u c", h=2, r=16)[h],
                  src_fn(h), reads=list(read_bufs), writes=[bounce_b], stream="bnc")
        return k.dma("sync", flat_t[:].rearrange("p (r c) -> p r c", r=16),
                     bounce.rearrange("p (r c) -> p r c", r=16),
                     reads=[bounce_b], writes=[flat_b], stream="bnc")

    def extract2(src_fn, read_bufs):
        for h in range(2):
            k.dma("sync", bounce.rearrange("(u h) (r c) -> h r u c", h=2, r=16)[h],
                  src_fn(h), reads=list(read_bufs), writes=[bounce_b], stream="bnc")
        return k.dma("sync", flat2_t[:].rearrange("p (r c) -> p r c", r=16),
                     bounce.rearrange("p (r c) -> p r c", r=16),
                     reads=[bounce_b], writes=[flat2_b], stream="bnc")

    def insert(dst_fns, wbufs):
        k.dma("sync", bounce.rearrange("p (r c) -> p r c", r=16),
              flat_t[:].rearrange("p (r c) -> p r c", r=16),
              reads=[flat_b], writes=[bounce_b], stream="bnc")
        for h in range(2):
            for dst_fn in dst_fns:
                k.dma("sync", dst_fn(h),
                      bounce.rearrange("(u h) (r c) -> h r u c", h=2, r=16)[h],
                      reads=[bounce_b], writes=list(wbufs), stream="bnc")

    def gj16():
        emit_gj(k, flat_t[:].rearrange("p (w f) -> p w f", w=1), flat_b, 16, 1,
                scr_t, scr_b, gjm_t, gjm_b)

    extract(lambda h: G_t[h * 64:h * 64 + 16, :, 0:16], [G_b])
    gj16()
    insert([lambda h: inv_t[h * 64:h * 64 + 16, :, 0:16]], [inv_b])

    for KD in (16, 32, 48):
        KE = KD + 16
        for bank, u0 in ((0, 0), (1, NP // 2)):
            un = NP // 2

            def mm_t0(t, u0=u0, un=un, bank=bank, KD=KD):
                last = None
                for h in range(2):
                    for ui in range(un):
                        u = u0 + ui
                        last = t.matmul(
                            ps[bank][h * 64:h * 64 + KD, ts(ui, 16)],
                            inv_t[h * 64:h * 64 + KD, u, 0:KD],
                            G_t[h * 64:h * 64 + KD, u, KD:KD + 16],
                            start=True, stop=True, tile_position=(h * 64, h * 64))
                return last

            k.op("tensor", mm_t0, reads=[inv_b, G_b], writes=[psb[bank]])
            for h in range(2):
                k.op("vector", lambda v, u0=u0, un=un, bank=bank, h=h, KD=KD:
                     v.tensor_copy(t0_t[h * 64:h * 64 + KD, u0:u0 + un, :],
                                   ps[bank][h * 64:h * 64 + KD].rearrange(
                                       "p (u c) -> p u c", c=16)),
                     reads=[psb[bank]], writes=[t0_b])

        for bank, u0 in ((2, 0), (3, NP // 2)):
            un = NP // 2

            def mm_p(t, u0=u0, un=un, bank=bank, KD=KD):
                last = None
                for h in range(2):
                    for ui in range(un):
                        u = u0 + ui
                        last = t.matmul(
                            ps[bank][h * 64:h * 64 + 16, ts(ui, 16)],
                            G_t[h * 64:h * 64 + KD, u, KD:KD + 16],
                            t0_t[h * 64:h * 64 + KD, u, :],
                            start=True, stop=True, tile_position=(h * 64, h * 64))
                return last

            k.op("tensor", mm_p, reads=[G_b, t0_b], writes=[psb[bank]])
            for h in range(2):
                k.op("vector", lambda v, u0=u0, un=un, bank=bank, h=h:
                     v.tensor_copy(stg_t[h * 64:h * 64 + 16, u0:u0 + un, :],
                                   ps[bank][h * 64:h * 64 + 16].rearrange(
                                       "p (u c) -> p u c", c=16)),
                     reads=[psb[bank]], writes=[stg_b])

        extract(lambda h, KD=KD: stg_t[h * 64:h * 64 + 16, :, :], [stg_b])
        extract2(lambda h, KD=KD, KE=KE: G_t[h * 64 + KD:h * 64 + KE, :, KD:KD + 16], [G_b])
        k.op("vector", lambda v: v.tensor_sub(flat_t[:], flat2_t[:], flat_t[:]),
             reads=[flat2_b], writes=[flat_b])
        gj16()
        insert([lambda h: is16_t[h * 64:h * 64 + 16, :, :]], [is16_b])

        for bank, u0 in ((4, 0), (5, NP // 2)):
            un = NP // 2

            def mm_tt(t, u0=u0, un=un, bank=bank, KD=KD):
                last = None
                for h in range(2):
                    for ui in range(un):
                        u = u0 + ui
                        last = t.matmul(
                            ps[bank][h * 64:h * 64 + 16, ts(ui % 8, 64)][:, 0:KD],
                            G_t[h * 64:h * 64 + KD, u, KD:KD + 16],
                            inv_t[h * 64:h * 64 + KD, u, 0:KD],
                            start=True, stop=True, tile_position=(h * 64, h * 64))
                        if ui % 8 == 7 or ui == un - 1:
                            pass
                return last

            for w0 in range(0, un, 8):
                wn = min(8, un - w0)

                def mm_ttw(t, u0=u0, w0=w0, wn=wn, bank=bank, KD=KD):
                    last = None
                    for h in range(2):
                        for ui in range(wn):
                            u = u0 + w0 + ui
                            last = t.matmul(
                                ps[bank][h * 64:h * 64 + 16, ts(ui, 64)][:, 0:KD],
                                G_t[h * 64:h * 64 + KD, u, KD:KD + 16],
                                inv_t[h * 64:h * 64 + KD, u, 0:KD],
                                start=True, stop=True, tile_position=(h * 64, h * 64))
                    return last

                k.op("tensor", mm_ttw, reads=[G_b, inv_b], writes=[psb[bank]])
                for h in range(2):
                    k.op("vector", lambda v, u0=u0, w0=w0, wn=wn, bank=bank, KD=KD, h=h:
                         v.tensor_scalar(tt_t[h * 64:h * 64 + 16, u0 + w0:u0 + w0 + wn, 0:KD],
                                         ps[bank][h * 64:h * 64 + 16].rearrange(
                                             "p (u c) -> p u c", c=64)[:, 0:wn, 0:KD],
                                         -1.0, None, op0=ALU.mult),
                         reads=[psb[bank]], writes=[tt_b])

        k.op("vector", lambda v, KD=KD, KE=KE:
             v.tensor_copy(tt_t[:, :, KD:KE],
                           identcol_t[:, None, :].broadcast_to([P_, NP, 16])),
             reads=[identcol_b], writes=[tt_b])

        for w0 in range(0, NP, 8):
            wn = min(8, NP - w0)
            bq = 6
            bu = 7

            def mm_q2(t, w0=w0, wn=wn, bq=bq, KD=KD, KE=KE):
                last = None
                for h in range(2):
                    for ui in range(wn):
                        u = w0 + ui
                        last = t.matmul(
                            ps[bq][h * 64:h * 64 + 16, ts(ui, 64)][:, 0:KE],
                            is16_t[h * 64:h * 64 + 16, u, :],
                            tt_t[h * 64:h * 64 + 16, u, 0:KE],
                            start=True, stop=True, tile_position=(h * 64, h * 64))
                return last

            k.op("tensor", mm_q2, reads=[is16_b, tt_b], writes=[psb[bq]])
            for h in range(2):
                k.op("vector", lambda v, w0=w0, wn=wn, bq=bq, KE=KE, h=h:
                     v.tensor_copy(q2_t[h * 64:h * 64 + 16, w0:w0 + wn, 0:KE],
                                   ps[bq][h * 64:h * 64 + 16].rearrange(
                                       "p (u c) -> p u c", c=64)[:, 0:wn, 0:KE]),
                     reads=[psb[bq]], writes=[q2_b])

            def mm_upd(t, w0=w0, wn=wn, bu=bu, KE=KE):
                last = None
                for h in range(2):
                    for ui in range(wn):
                        u = w0 + ui
                        last = t.matmul(
                            ps[bu][h * 64:h * 64 + KE, ts(ui, 64)][:, 0:KE],
                            q2_t[h * 64:h * 64 + 16, u, 0:KE],
                            tt_t[h * 64:h * 64 + 16, u, 0:KE],
                            start=True, stop=True, tile_position=(h * 64, h * 64))
                return last

            k.op("tensor", mm_upd, reads=[q2_b, tt_b], writes=[psb[bu]])
            for h in range(2):
                k.op("vector", lambda v, w0=w0, wn=wn, bu=bu, h=h, KE=KE:
                     v.tensor_add(inv_t[h * 64:h * 64 + KE, w0:w0 + wn, 0:KE],
                                  inv_t[h * 64:h * 64 + KE, w0:w0 + wn, 0:KE],
                                  ps[bu][h * 64:h * 64 + KE].rearrange(
                                      "p (u c) -> p u c", c=64)[:, 0:wn, 0:KE]),
                     reads=[psb[bu]], writes=[inv_b])


def ts(i, n):
    return slice(i * n, (i + 1) * n)


def breg(g, RB, v):
    if v not in RB:
        RB[v] = g.to_reg(v)
    return RB[v]


def build(phase=2, debug=True, ffn_stop=99, probe=False):
    nc = bass.Bass("TRN2", target_bir_lowering=False)
    k = K(nc)
    st = ExitStack()

    def din(name, shape, dt=F32):
        return nc.dram_tensor(name, shape, dt, kind="ExternalInput").ap()

    def dout(name, shape, dt=F32):
        return nc.dram_tensor(name, shape, dt, kind="ExternalOutput").ap()

    def sb(stk, name, shape, dt, **kw):
        t = stk.enter_context(nc.sbuf_tensor(name, shape, dt, **kw))
        return t, Buf(name)

    x_in = din("x", [T, D])
    wq_in = din("wq", [D, D])
    wk_in = din("wk", [D, D])
    wv_in = din("wv", [D, D])
    lng_in = din("lng", [128, D])
    lnb_in = din("lnb", [128, D])
    epse_in = din("epse", [128, 512])
    wo_in = din("wo", [D, D])
    coef_in = din("coef", [128, 8])
    gjm_in = din("gjm", [128, 16, 16])
    gjm8_in = din("gjm8", [128, 8, 8])
    idc_in = din("idc", [128, 16])

    qf_dram = nc.dram_tensor("qf_scr", [8, 128, T], BF16, kind="Internal").ap()
    qfd_b = Buf("qf_dram")
    wr_dram = nc.dram_tensor("wr_scr", [4, 128, 64, 64], BF16, kind="Internal").ap()
    wrd_b = Buf("wr_dram")
    wbf_dram = nc.dram_tensor("wbf_scr", [3, 128, 8, D], BF16, kind="Internal").ap()
    wbfd_b = Buf("wbf_dram")
    bounce = nc.dram_tensor("bounce", [128, 256], F32, kind="Internal").ap()
    bounce_b = Buf("bounce")

    TT = 512
    NT = T // TT
    GP = TT // 128

    final = []

    lng_t, lng_b = sb(st, "lng_t", [128, D], F32)
    lnb_t, lnb_b = sb(st, "lnb_t", [128, D], F32)
    stats_all, statsall_b = sb(st, "stats_all", [128, 32, 2], F32)

    if debug:
        dbg_xn2 = dout("dbg_xn2", [T, D], BF16)
        dbg_ginv = dout("dbg_ginv", [128, 64, 64], F32)
        dbg_wr = dout("dbg_wr", [4, 128, 64, 64], BF16)
        if phase >= 3:
            dbg_o = dout("dbg_o", [8, 128, T], BF16)
            dbg_x2 = dout("dbg_x2", [T, D], F32)
            dbg_h2 = dout("dbg_h2", [T, D], BF16)

    k.dma("sync", lng_t[:], lng_in, writes=[lng_b], stream="w0")
    k.dma("sync", lnb_t[:], lnb_in, writes=[lnb_b], stream="w0")

    with nc.sbuf_tensor("wcast", [128, 8, D], BF16) as wcast_t:
        wcast_b = Buf("wcast")
        for i, w_in in enumerate((wq_in, wk_in, wv_in)):
            k.dma("gpsimd", wcast_t[:], w_in.rearrange("(c p) d -> p c d", p=128),
                  writes=[wcast_b], stream="gw")
            k.dma("gpsimd", wbf_dram[i], wcast_t[:], reads=[wcast_b],
                  writes=[wbfd_b], stream="gw")

    p_dram = nc.dram_tensor("p_dram", [8 * T, D], F32, kind="Internal").ap()
    pd_b = Buf("p_dram")

    p1 = ExitStack()
    wA_t, wA_b = sb(p1, "wA", [128, 8, D], BF16)
    wB_t, wB_b = sb(p1, "wB", [128, 8, D], BF16)
    x_t, x_b = sb(p1, "x_t", [128, D], F32)
    sc1_t, sc1_b = sb(p1, "sc1", [128, D], F32)
    m1_t, m1_b = sb(p1, "m1", [128, D], F32)
    stat_t, stat_b = sb(p1, "stat", [128, GP, 8], F32)
    xn2_t, xn2_b = sb(p1, "xn2", [128, D], BF16)
    xn2f, xn2f_b = sb(p1, "xn2f", [128, 8, TT], BF16, is_transpose_dest=True)
    qtile, qtile_b = sb(p1, "qtile", [128, 8, TT], BF16)
    kv_t, kv_b = sb(p1, "kv", [128, 2, D], F32)
    G_t, G_b = sb(p1, "G_t", [128, 64, 64], F32)
    B_t, B_b = sb(p1, "B_t", [128, 64, 64], BF16)
    inv_t, inv_b = sb(p1, "inv_t", [128, 64, 64], F32)
    epse_t, epse_b = sb(p1, "epse_t", [128, 512], F32)
    gjm_t, gjm_b = sb(p1, "gjm_t", [128, 16, 16], F32)
    gjm8_t, gjm8_b = sb(p1, "gjm8_t", [128, 8, 8], F32)
    idc_t, idc_b = sb(p1, "idc_t", [128, 16], F32)
    scrgj_t, scrgj_b = sb(p1, "scrgj", [128, 300], F32)

    ps, psb = [], []
    for i in range(8):
        ps.append(st.enter_context(nc.psum_tensor(f"ps{i}", [128, 512], F32)))
        psb.append(Buf(f"ps{i}"))

    zzp_t, zzp_b = sb(p1, "zzp", [128, 2, D], F32)
    k.op("vector", lambda v: v.memzero(zzp_t[:]), writes=[zzp_b])
    for a in range(128):
        k.dma("gpsimd", p_dram.rearrange("(a p) d -> a p d", p=128)[2 * a:2 * a + 2]
              .rearrange("a p d -> p a d"),
              zzp_t[:], reads=[zzp_b], writes=[pd_b], stream="pz")

    k.dma("sync", epse_t[:], epse_in, writes=[epse_b], stream="w0")
    k.dma("sync", gjm_t[:], gjm_in, writes=[gjm_b], stream="w0")
    k.dma("sync", gjm8_t[:], gjm8_in, writes=[gjm8_b], stream="w0")
    k.dma("sync", idc_t[:], idc_in, writes=[idc_b], stream="w0")

    def emit_batch(bb):
        k.mark(f"inv{bb}")
        k.op("vector", lambda v: v.memzero(inv_t[:]), writes=[inv_b])
        with nc.sbuf_tensor(f"t0_{bb}", [128, 64, 16], F32) as t0_t, \
             nc.sbuf_tensor(f"tt_{bb}", [128, 64, 64], F32) as tt_t, \
             nc.sbuf_tensor(f"stg_{bb}", [128, 64, 16], F32) as stg_t, \
             nc.sbuf_tensor(f"is16_{bb}", [128, 64, 16], F32) as is16_t, \
             nc.sbuf_tensor(f"q2_{bb}", [128, 64, 64], F32) as q2_t, \
             nc.sbuf_tensor(f"flat_{bb}", [128, 256], F32) as flat_t, \
             nc.sbuf_tensor(f"flat2_{bb}", [128, 256], F32) as flat2_t:
            t0_b, tt_b, stg_b = Buf("t0"), Buf("tt"), Buf("stg")
            is16_b, q2_b, flat_b, flat2_b = Buf("is16"), Buf("q2"), Buf("flat"), Buf("flat2")
            emit_border_inv(k, G_t, G_b, inv_t, inv_b, t0_t, t0_b, tt_t, tt_b,
                                 stg_t, stg_b, is16_t, is16_b, q2_t, q2_b,
                                 flat_t, flat_b, flat2_t, flat2_b, scrgj_t, scrgj_b,
                                 gjm_t, gjm_b, idc_t, idc_b, bounce, bounce_b,
                                 ps, psb, npair=64)
        k.mark(f"wr{bb}")
        if debug and bb == 0:
            final.append(k.dma("gpsimd", dbg_ginv, inv_t[:], reads=[inv_b], stream="dbg"))

        with nc.sbuf_tensor(f"invb_{bb}", [128, 64, 64], BF16) as invb_t, \
             nc.sbuf_tensor(f"zb_{bb}", [128, 64, 64], BF16) as zb_t, \
             nc.sbuf_tensor(f"g8b_{bb}", [128, 64, 8], BF16) as g8b_t, \
             nc.sbuf_tensor(f"w_{bb}", [128, 64, 64], BF16) as w_t, \
             nc.sbuf_tensor(f"wt_{bb}", [128, 64, 64], BF16) as wt_t, \
             nc.sbuf_tensor(f"m2_{bb}", [128, 64, 8], BF16) as m2_t, \
             nc.sbuf_tensor(f"m2t_{bb}", [128, 64, 64], BF16) as m2t_t, \
             nc.sbuf_tensor(f"g8_{bb}", [128, 64, 8], F32) as g8_t, \
             nc.sbuf_tensor(f"y_{bb}", [128, 64, 64], BF16) as y_t, \
             nc.sbuf_tensor(f"wr_{bb}", [128, 64, 64], BF16) as wrb_t, \
             nc.sbuf_tensor(f"flg8_{bb}", [128, 64], F32) as flg8_t:
            w_b, wt_b, m2_b, m2t_b = Buf("w"), Buf("wt"), Buf("m2"), Buf("m2t")
            g8_b, z_b, y_b, wrb_b, flg8_b = Buf("g8"), Buf("z"), Buf("y"), Buf("wrb"), Buf("flg8")
            invb_b, zb_b, g8b_b = Buf("invb"), Buf("zb"), Buf("g8b")
            k.op("vector", lambda v: v.tensor_copy(invb_t[:], inv_t[:]),
                 reads=[inv_b], writes=[invb_b])

            def waves(mm_fn, evac_fn, rbufs, banks=(0, 1)):
                for w0 in range(0, 64, 8):
                    bank = banks[(w0 // 8) % 2]
                    k.op("tensor", lambda t, w0=w0, bank=bank: mm_fn(t, w0, bank),
                         reads=list(rbufs), writes=[psb[bank]])
                    evac_fn(w0, bank)

            def full_mm(lhs_fn, rhs_fn, M):
                def mm(t, w0, bank):
                    last = None
                    for h in range(2):
                        for ui in range(8):
                            u = w0 + ui
                            last = t.matmul(
                                ps[bank][h * 64:h * 64 + M, ts(ui, 64)],
                                lhs_fn(h, u), rhs_fn(h, u),
                                start=True, stop=True, tile_position=(h * 64, h * 64))
                    return last
                return mm

            def ev_full(dst_t, dst_b):
                def ev(w0, bank):
                    k.op("vector", lambda v, w0=w0, bank=bank:
                         v.tensor_copy(dst_t[:, w0:w0 + 8, :].rearrange("p u c -> p (u c)"),
                                       ps[bank][:]),
                         reads=[psb[bank]], writes=[dst_b])
                return ev

            def ev_rows8(dst_t, dst_b):
                def ev(w0, bank):
                    for h in range(2):
                        k.op("vector", lambda v, w0=w0, bank=bank, h=h:
                             v.tensor_copy(dst_t[h * 64:h * 64 + 8, w0:w0 + 8, :],
                                           ps[bank][h * 64:h * 64 + 8].rearrange(
                                               "p (u c) -> p u c", c=64)),
                             reads=[psb[bank]], writes=[dst_b])
                return ev

            waves(full_mm(lambda h, u: invb_t[h * 64:(h + 1) * 64, u, :],
                          lambda h, u: B_t[h * 64:(h + 1) * 64, u, :], 64),
                  ev_full(w_t, w_b), [invb_b, B_b])
            waves(full_mm(lambda h, u: B_t[h * 64:(h + 1) * 64, u, :],
                          lambda h, u: invb_t[h * 64:(h + 1) * 64, u, :], 64),
                  ev_full(wt_t, wt_b), [invb_b, B_b], banks=(2, 3))

            def mmM2(t):
                last = None
                for h in range(2):
                    for u in range(64):
                        last = t.matmul(ps[4][h * 64:(h + 1) * 64, ts(u, 8)],
                                        wt_t[h * 64:(h + 1) * 64, u, :],
                                        w_t[h * 64:(h + 1) * 64, u, 0:8],
                                        start=True, stop=True,
                                        tile_position=(h * 64, h * 64))
                return last

            k.op("tensor", mmM2, reads=[wt_b, w_b], writes=[psb[4]])
            k.op("vector", lambda v:
                 v.tensor_copy(m2_t[:].rearrange("p u c -> p (u c)"), ps[4][:]),
                 reads=[psb[4]], writes=[m2_b])

            waves(full_mm(lambda h, u: w_t[h * 64:(h + 1) * 64, u, 0:8],
                          lambda h, u: wt_t[h * 64:(h + 1) * 64, u, :], 8),
                  ev_rows8(m2t_t, m2t_b), [w_b, wt_b], banks=(5, 6))

            def mmG8(t):
                last = None
                for h in range(2):
                    for u in range(64):
                        last = t.matmul(ps[7][h * 64:h * 64 + 8, ts(u, 8)],
                                        m2_t[h * 64:(h + 1) * 64, u, :],
                                        m2_t[h * 64:(h + 1) * 64, u, :],
                                        start=True, stop=True,
                                        tile_position=(h * 64, h * 64))
                return last

            k.op("tensor", mmG8, reads=[m2_b], writes=[psb[7]])
            for h in range(2):
                k.op("vector", lambda v, h=h:
                     v.tensor_copy(g8_t[h * 64:h * 64 + 8, :, :],
                                   ps[7][h * 64:h * 64 + 8].rearrange("p (u c) -> p u c", c=8)),
                     reads=[psb[7]], writes=[g8_b])

            for h in range(2):
                k.dma("sync", bounce.rearrange("(u h) (r c) -> h r u c", h=2, r=16)[h, 0:8, :, 0:8],
                      g8_t[h * 64:h * 64 + 8, :, :],
                      reads=[g8_b], writes=[bounce_b], stream="bnc")
            k.dma("sync", flg8_t[:].rearrange("p (r c) -> p r c", r=8),
                  bounce.rearrange("p (r c) -> p r c", r=16)[:, 0:8, 0:8],
                  reads=[bounce_b], writes=[flg8_b], stream="bnc")
            emit_gj(k, flg8_t[:].rearrange("p (w f) -> p w f", w=1), flg8_b, 8, 1,
                         scrgj_t, scrgj_b, gjm8_t, gjm8_b)
            k.dma("sync", bounce.rearrange("p (r c) -> p r c", r=16)[:, 0:8, 0:8],
                  flg8_t[:].rearrange("p (r c) -> p r c", r=8),
                  reads=[flg8_b], writes=[bounce_b], stream="bnc")
            for h in range(2):
                k.dma("sync", g8_t[h * 64:h * 64 + 8, :, :],
                      bounce.rearrange("(u h) (r c) -> h r u c", h=2, r=16)[h, 0:8, :, 0:8],
                      reads=[bounce_b], writes=[g8_b], stream="bnc")

            waves(full_mm(lambda h, u: m2_t[h * 64:(h + 1) * 64, u, :],
                          lambda h, u: w_t[h * 64:(h + 1) * 64, u, :], 8),
                  ev_rows8(zb_t, zb_b), [m2_b, w_b], banks=(0, 1))

            k.op("vector", lambda v: v.tensor_copy(g8b_t[:], g8_t[:]),
                 reads=[g8_b], writes=[g8b_b])
            waves(full_mm(lambda h, u: g8b_t[h * 64:h * 64 + 8, u, :],
                          lambda h, u: zb_t[h * 64:h * 64 + 8, u, :], 8),
                  ev_rows8(y_t, y_b), [g8b_b, zb_b], banks=(2, 3))

            waves(full_mm(lambda h, u: m2t_t[h * 64:h * 64 + 8, u, :],
                          lambda h, u: y_t[h * 64:h * 64 + 8, u, :], 64),
                  ev_full(wrb_t, wrb_b), [m2t_b, y_b], banks=(4, 5))

            k.dma("sync", wr_dram[bb], wrb_t[:], reads=[wrb_b], writes=[wrd_b], stream="wr")
            if debug:
                final.append(k.dma("gpsimd", dbg_wr[bb], wrb_t[:],
                                   reads=[wrb_b], stream="dbg"))

    for t_i in range(NT):
        k.mark(f"tl{t_i}")
        k.dma("sync", wA_t[:], wbf_dram[0], reads=[wbfd_b], writes=[wA_b], stream="wld")
        for gi in range(GP):
            gg = t_i * GP + gi
            xsrc = x_in.rearrange("(s p) d -> s p d", p=128)[gg]
            k.dma("sync", x_t[:], xsrc, writes=[x_b], stream="x0")
            k.op("vector",
                 lambda v, gi=gi: v.reduce_sum(stat_t[:, gi, 0:1], x_t[:], axis=AX.X),
                 reads=[x_b], writes=[stat_b])
            k.op("scalar",
                 lambda s, gi=gi: s.activation(sc1_t[:], x_t[:], AF.Square,
                                               accum_out=stat_t[:, gi, 1:2]),
                 reads=[x_b], writes=[sc1_b, stat_b])
            CC = lambda j, gi=gi: stat_t[:, gi, j:j + 1]
            k.op("vector", lambda v, CC=CC: v.tensor_scalar_mul(CC(2), CC(0), 1.0 / D),
                 reads=[stat_b], writes=[stat_b])
            k.op("vector", lambda v, CC=CC: v.tensor_scalar_mul(CC(3), CC(1), 1.0 / D),
                 reads=[stat_b], writes=[stat_b])
            k.op("vector", lambda v, CC=CC: v.tensor_mul(CC(1), CC(2), CC(2)),
                 reads=[stat_b], writes=[stat_b])
            k.op("vector", lambda v, CC=CC: v.tensor_sub(CC(3), CC(3), CC(1)),
                 reads=[stat_b], writes=[stat_b])
            k.op("vector", lambda v, CC=CC: v.tensor_scalar_add(CC(4), CC(3), EPS_LN),
                 reads=[stat_b], writes=[stat_b])
            k.op("scalar", lambda s, CC=CC: s.sqrt(CC(0), CC(4)),
                 reads=[stat_b], writes=[stat_b])
            k.op("vector", lambda v, CC=CC: v.reciprocal(CC(5), CC(0)),
                 reads=[stat_b], writes=[stat_b])
            k.op("vector", lambda v, CC=CC: v.tensor_mul(CC(1), CC(5), CC(5)),
                 reads=[stat_b], writes=[stat_b])
            k.op("vector", lambda v, CC=CC: v.tensor_mul(CC(1), CC(1), CC(3)),
                 reads=[stat_b], writes=[stat_b])
            k.op("vector", lambda v, CC=CC: v.tensor_scalar_add(CC(1), CC(1), EPS_LN),
                 reads=[stat_b], writes=[stat_b])
            k.op("scalar", lambda s, CC=CC: s.sqrt(CC(0), CC(1)),
                 reads=[stat_b], writes=[stat_b])
            k.op("vector", lambda v, CC=CC: v.reciprocal(CC(0), CC(0)),
                 reads=[stat_b], writes=[stat_b])
            k.op("vector", lambda v, CC=CC: v.tensor_mul(CC(6), CC(5), CC(0)),
                 reads=[stat_b], writes=[stat_b])
            k.op("vector", lambda v, CC=CC: v.tensor_mul(CC(7), CC(2), CC(6)),
                 reads=[stat_b], writes=[stat_b])
            k.op("vector", lambda v, CC=CC: v.tensor_scalar_mul(CC(7), CC(7), -1.0),
                 reads=[stat_b], writes=[stat_b])
            k.op("vector", lambda v, gg=gg, CC=CC: v.tensor_copy(stats_all[:, gg, 0:1], CC(2)),
                 reads=[stat_b], writes=[statsall_b])
            k.op("vector", lambda v, gg=gg, CC=CC: v.tensor_copy(stats_all[:, gg, 1:2], CC(5)),
                 reads=[stat_b], writes=[statsall_b])
            k.op("gpsimd",
                 lambda g, gi=gi: g.tensor_scalar(sc1_t[:], x_t[:],
                                                  stat_t[:, gi, 6:7], stat_t[:, gi, 7:8],
                                                  op0=ALU.mult, op1=ALU.add),
                 reads=[x_b, stat_b], writes=[sc1_b])
            k.op("gpsimd", lambda g: g.tensor_mul(m1_t[:], sc1_t[:], lng_t[:]),
                 reads=[sc1_b, lng_b], writes=[m1_b])
            k.op("gpsimd", lambda g: g.tensor_add(xn2_t[:], m1_t[:], lnb_t[:]),
                 reads=[m1_b, lnb_b], writes=[xn2_b])
            k.dma("sync", xn2f[:, :, ts(gi, 128)], xn2_t[:],
                  reads=[xn2_b], writes=[xn2f_b], stream="tr", transpose=True)
            if debug:
                final.append(k.dma("gpsimd",
                      dbg_xn2.rearrange("(s p) d -> s p d", p=128)[gg],
                      xn2_t[:], reads=[xn2_b], stream="dbg"))

        for m in range(8):
            bank = m % 4

            def mm_q(t, m=m, bank=bank):
                last = None
                for kk in range(8):
                    last = t.matmul(ps[bank][:, :], wA_t[:, kk, ts(m, 128)],
                                    xn2f[:, kk, :], start=(kk == 0), stop=(kk == 7))
                return last

            k.op("tensor", mm_q, reads=[wA_b, xn2f_b], writes=[psb[bank]])
            k.op("scalar", lambda s, m=m, bank=bank: s.copy(qtile[:, m, :], ps[bank][:]),
                 reads=[psb[bank]], writes=[qtile_b])
        k.dma("sync", qf_dram[:, :, ts(t_i, TT)].rearrange("c p n -> p c n"), qtile[:],
              reads=[qtile_b], writes=[qfd_b], stream="q0")

        k.dma("sync", wB_t[:], wbf_dram[1], reads=[wbfd_b], writes=[wB_b], stream="wld")
        k.dma("sync", wA_t[:], wbf_dram[2], reads=[wbfd_b], writes=[wA_b], stream="wld")

        for gi in range(GP):
            for half in range(2):
                bank = 4 + half

                def mm_k(t, gi=gi, half=half, bank=bank):
                    last = None
                    for kk in range(8):
                        last = t.matmul(ps[bank][:, :], xn2f[:, kk, ts(gi, 128)],
                                        wB_t[:, kk, ts(half, 512)],
                                        start=(kk == 0), stop=(kk == 7))
                    return last

                k.op("tensor", mm_k, reads=[wB_b, xn2f_b], writes=[psb[bank]])
                k.op("vector",
                     lambda v, half=half, bank=bank:
                     v.tensor_copy(kv_t[:, 0, ts(half, 512)], ps[bank][:]),
                     reads=[psb[bank]], writes=[kv_b])

                def mm_v(t, gi=gi, half=half, bank=bank):
                    last = None
                    for kk in range(8):
                        last = t.matmul(ps[bank + 2][:, :], xn2f[:, kk, ts(gi, 128)],
                                        wA_t[:, kk, ts(half, 512)],
                                        start=(kk == 0), stop=(kk == 7))
                    return last

                k.op("tensor", mm_v, reads=[wA_b, xn2f_b], writes=[psb[bank + 2]])
                k.op("vector",
                     lambda v, half=half, bank=bank:
                     v.tensor_copy(kv_t[:, 1, ts(half, 512)], ps[bank + 2][:]),
                     reads=[psb[bank + 2]], writes=[kv_b])

            if phase >= 2:
                pos = (t_i % 2) * GP + gi
                bkG, bkB = pos % 2, 2 + pos % 2

                def mm_g(t, bkG=bkG):
                    last = None
                    for lane in range(16):
                        h, lh = lane % 2, lane // 2
                        last = t.matmul(
                            ps[bkG][h * 64:(h + 1) * 64, ts(lh, 64)],
                            kv_t[:, 0, ts(lane, 64)], kv_t[:, 0, ts(lane, 64)],
                            start=True, stop=True, tile_position=(0, h * 64))
                    return last

                k.op("tensor", mm_g, reads=[kv_b], writes=[psb[bkG]])
                k.op("vector", lambda v, pos=pos, bkG=bkG:
                     v.tensor_add(G_t[:, ts(pos, 8), :].rearrange("p u c -> p (u c)"),
                                  ps[bkG][:], epse_t[:]),
                     reads=[psb[bkG], epse_b], writes=[G_b])

                def mm_b(t, bkB=bkB):
                    last = None
                    for lane in range(16):
                        h, lh = lane % 2, lane // 2
                        last = t.matmul(
                            ps[bkB][h * 64:(h + 1) * 64, ts(lh, 64)],
                            kv_t[:, 0, ts(lane, 64)], kv_t[:, 1, ts(lane, 64)],
                            start=True, stop=True, tile_position=(0, h * 64))
                    return last

                k.op("tensor", mm_b, reads=[kv_b], writes=[psb[bkB]])
                k.op("vector", lambda v, pos=pos, bkB=bkB:
                     v.tensor_copy(B_t[:, ts(pos, 8), :].rearrange("p u c -> p (u c)"),
                                   ps[bkB][:]),
                     reads=[psb[bkB]], writes=[B_b])

        if phase >= 2 and t_i % 2 == 1:
            emit_batch(t_i // 2)

    p1.close()
    k.fence()

    if phase >= 3:
        k.mark("phase3")
        L = dict(locals())
        emit_phase3(nc, k, st, final, debug, L)
        if phase >= 4:
            emit_ffn(nc, k, st, final, debug, L, stop=L.get("ffn_stop", 99) if isinstance(L, dict) else 99)

    k.finalize(final)
    st.close()
    return nc





def emit_phase3(nc, k, st, final, debug, L):
    sb = L["sb"]
    ps, psb = L["ps"], L["psb"]
    wr_dram, wrd_b = L["wr_dram"], L["wrd_b"]
    qf_dram, qfd_b = L["qf_dram"], L["qfd_b"]
    x_in = L["x_in"]
    wo_in = L["wo_in"]
    coef_in = L["coef_in"]
    stats_all, statsall_b = L["stats_all"], L["statsall_b"]
    dout = L["dout"]
    din = L["din"]
    if debug:
        dbg_o, dbg_x2, dbg_h2 = L["dbg_o"], L["dbg_x2"], L["dbg_h2"]

    p2 = ExitStack()
    wo_t, wo_b = sb(p2, "wo_t", [128, 8, D], BF16)
    coef_t, coef_b = sb(p2, "coef_t", [128, 8], F32)
    sstate, sstate_b = sb(p2, "sstate", [128, 8, 64], F32)
    s_all, sall_b = sb(p2, "s_all", [128, 2, 512], BF16)
    s_dram = nc.dram_tensor("s_scr", [32, 128, 512], BF16, kind="Internal").ap()
    sd_b = Buf("s_dram")
    carry, carry_b = sb(p2, "carry", [128, 512], F32)
    fj, fj_b = sb(p2, "fj", [128, 512], F32)
    scorr, scorr_b = sb(p2, "scorr", [128, 8, 64], BF16)
    wrw, wrw_b = sb(p2, "wrw", [128, 2, 8, 64], BF16)
    qt_t, qt_b = sb(p2, "qt", [128, 8, 128], BF16)
    o_t, o_b = sb(p2, "o_t", [128, 8, T], BF16)

    ag1_in = nc.dram_tensor("ag1_in", [128, 512], F32, kind="Internal").ap()
    ag1_out = nc.dram_tensor("ag1_out", [8, 128, 512], F32, kind="Internal").ap()
    ag1i_b, ag1o_b = Buf("ag1_in"), Buf("ag1_out")
    x2_dram = nc.dram_tensor("x2_scr", [T, D], F32, kind="Internal").ap()
    x2d_b = Buf("x2_dram")
    h2_dram = nc.dram_tensor("h2_scr", [T, D], BF16, kind="Internal").ap()
    h2d_b = Buf("h2_dram")

    k.dma("gpsimd", wo_t[:], wo_in.rearrange("(c p) d -> p c d", p=128),
          writes=[wo_b], stream="gw")
    k.dma("sync", coef_t[:], coef_in, writes=[coef_b], stream="w0")

    k.mark("scan")
    k.op("vector", lambda v: v.memzero(sstate[:]), writes=[sstate_b])
    wrwb2 = [Buf("wrw0"), Buf("wrw1")]
    for c in range(32):
        bb, pos = c // 8, c % 8
        k.dma("sync", wrw[:, c % 2], wr_dram[bb][:, ts(pos, 8), :], reads=[wrd_b],
              writes=[wrwb2[c % 2]], stream="wrl")
        k.op("vector", lambda v, c=c: v.scalar_tensor_tensor(
                 sstate[:].rearrange("p a b -> p (a b)"),
                 sstate[:].rearrange("p a b -> p (a b)"), ALPHA,
                 wrw[:, c % 2].rearrange("p a b -> p (a b)"),
                 op0=ALU.mult, op1=ALU.add),
             reads=[wrwb2[c % 2], sstate_b], writes=[sstate_b])
        k.op("vector", lambda v, c=c: v.tensor_copy(
                 s_all[:, c % 2, :], sstate[:].rearrange("p a b -> p (a b)")),
             reads=[sstate_b], writes=[sall_b])
        k.dma("sync", s_dram[c], s_all[:, c % 2, :], reads=[sall_b], writes=[sd_b],
              stream="ss")

    k.dma("sync", ag1_in, sstate[:].rearrange("p a b -> p (a b)"),
          reads=[sstate_b], writes=[ag1i_b], stream="ag")
    k.op("gpsimd", lambda g: g.collective_compute(
             "AllGather", ALU.bypass, ins=[ag1_in], outs=[ag1_out],
             replica_groups=[[0, 1, 2, 3, 4, 5, 6, 7]]),
         reads=[ag1i_b], writes=[ag1o_b])

    k.op("vector", lambda v: v.memzero(carry[:]), writes=[carry_b])
    for j in range(8):
        k.dma("sync", fj[:], ag1_out[j], reads=[ag1o_b], writes=[fj_b], stream="agl")
        k.op("vector", lambda v, j=j: v.scalar_tensor_tensor(
                 carry[:], fj[:], coef_t[:, j:j + 1], carry[:],
                 op0=ALU.mult, op1=ALU.add),
             reads=[fj_b, coef_b, carry_b], writes=[carry_b])

    k.mark("out2")
    for c in range(32):
        k.dma("sync", s_all[:, c % 2, :], s_dram[c], reads=[sd_b], writes=[sall_b],
              stream="ss")
        k.op("vector", lambda v, c=c: v.scalar_tensor_tensor(
                 scorr[:].rearrange("p a b -> p (a b)"),
                 carry[:], float(ALPHA ** (c + 1)), s_all[:, c % 2, :],
                 op0=ALU.mult, op1=ALU.add),
             reads=[carry_b, sall_b], writes=[scorr_b])
        k.dma("sync", qt_t[:], qf_dram[:, :, ts(c, 128)].rearrange("c p n -> p c n"),
              reads=[qfd_b], writes=[qt_b], stream="ql")
        for bi in range(4):
            def mm_o(t, c=c, bi=bi):
                last = None
                for sl in range(2):
                    lh = bi + sl * 4
                    for h in range(2):
                        last = t.matmul(
                            ps[bi][h * 64:(h + 1) * 64, ts(sl, 128)],
                            scorr[h * 64:(h + 1) * 64, lh, :],
                            qt_t[h * 64:(h + 1) * 64, lh, :],
                            start=True, stop=True, tile_position=(h * 64, h * 64))
                return last

            k.op("tensor", mm_o, reads=[scorr_b, qt_b], writes=[psb[bi]])
            k.op("vector", lambda v, c=c, bi=bi: v.tensor_scalar(
                     o_t[:, bi:8:4, ts(c, 128)],
                     ps[bi][:].rearrange("p (sl n) -> p sl n", n=128)[:, 0:2, :],
                     SCALE, None, op0=ALU.mult),
                 reads=[psb[bi]], writes=[o_b])

    if debug:
        final.append(k.dma("gpsimd", dbg_o.rearrange("c p n -> p c n"), o_t[:],
                           reads=[o_b], stream="dbg"))

    nmr, nmr_b = sb(p2, "nmr", [128, 32], F32)
    k.op("vector", lambda v: v.tensor_mul(nmr[:], stats_all[:, :, 0], stats_all[:, :, 1]),
         reads=[statsall_b], writes=[nmr_b])
    k.op("vector", lambda v: v.tensor_scalar_mul(nmr[:], nmr[:], -1.0),
         reads=[nmr_b], writes=[nmr_b])

    k.mark("x2h2")
    xr_t, xr_b = sb(p2, "xr", [128, D], F32)
    hh_t, hh_b = sb(p2, "hh", [128, D], F32)
    x2_t, x2_b = sb(p2, "x2t", [128, D], F32)
    h2_t, h2_b = sb(p2, "h2t", [128, D], BF16)
    st2_t, st2_b = sb(p2, "st2", [128, 8], F32)
    sq2_t, sq2_b = sb(p2, "sq2", [128, D], F32)

    for tb in range(32):
        k.dma("sync", xr_t[:], x_in.rearrange("(s p) d -> s p d", p=128)[tb],
              writes=[xr_b], stream="x2l")
        for dh in range(2):
            bank = 4 + dh

            def mm_wo(t, tb=tb, dh=dh, bank=bank):
                last = None
                for fc in range(8):
                    last = t.matmul(ps[bank][:, :], o_t[:, fc, ts(tb, 128)],
                                    wo_t[:, fc, ts(dh, 512)],
                                    start=(fc == 0), stop=(fc == 7))
                return last

            k.op("tensor", mm_wo, reads=[o_b, wo_b], writes=[psb[bank]])
        k.op("gpsimd", lambda g, tb=tb: g.tensor_scalar(
                 hh_t[:], xr_t[:], stats_all[:, tb, 1:2], nmr[:, tb:tb + 1],
                 op0=ALU.mult, op1=ALU.add),
             reads=[xr_b, statsall_b, nmr_b], writes=[hh_b])
        k.op("gpsimd", lambda g: g.tensor_add(hh_t[:], hh_t[:], xr_t[:]),
             reads=[xr_b, hh_b], writes=[hh_b])
        for dh in range(2):
            bank = 4 + dh
            k.op("vector", lambda v, dh=dh, bank=bank: v.tensor_add(
                     x2_t[:, ts(dh, 512)], hh_t[:, ts(dh, 512)], ps[bank][:]),
                 reads=[hh_b, psb[bank]], writes=[x2_b])
        k.dma("sync", x2_dram.rearrange("(s p) d -> s p d", p=128)[tb], x2_t[:],
              reads=[x2_b], writes=[x2d_b], stream="x2s")
        k.op("vector", lambda v: v.reduce_sum(st2_t[:, 0:1], x2_t[:], axis=AX.X),
             reads=[x2_b], writes=[st2_b])
        k.op("scalar", lambda s: s.activation(sq2_t[:], x2_t[:], AF.Square,
                                              accum_out=st2_t[:, 1:2]),
             reads=[x2_b], writes=[sq2_b, st2_b])
        C2 = lambda j: st2_t[:, j:j + 1]
        k.op("vector", lambda v: v.tensor_scalar_mul(C2(2), C2(0), 1.0 / D),
             reads=[st2_b], writes=[st2_b])
        k.op("vector", lambda v: v.tensor_scalar_mul(C2(3), C2(1), 1.0 / D),
             reads=[st2_b], writes=[st2_b])
        k.op("vector", lambda v: v.tensor_mul(C2(1), C2(2), C2(2)),
             reads=[st2_b], writes=[st2_b])
        k.op("vector", lambda v: v.tensor_sub(C2(3), C2(3), C2(1)),
             reads=[st2_b], writes=[st2_b])
        k.op("vector", lambda v: v.tensor_scalar_add(C2(4), C2(3), EPS_LN),
             reads=[st2_b], writes=[st2_b])
        k.op("scalar", lambda s: s.sqrt(C2(5), C2(4)), reads=[st2_b], writes=[st2_b])
        k.op("vector", lambda v: v.reciprocal(C2(5), C2(5)), reads=[st2_b], writes=[st2_b])
        k.op("vector", lambda v: v.tensor_mul(C2(6), C2(2), C2(5)),
             reads=[st2_b], writes=[st2_b])
        k.op("vector", lambda v: v.tensor_scalar_mul(C2(6), C2(6), -1.0),
             reads=[st2_b], writes=[st2_b])
        k.op("gpsimd", lambda g: g.tensor_scalar(
                 h2_t[:], x2_t[:], C2(5), C2(6), op0=ALU.mult, op1=ALU.add),
             reads=[x2_b, st2_b], writes=[h2_b])
        k.dma("sync", h2_dram.rearrange("(s p) d -> s p d", p=128)[tb], h2_t[:],
              reads=[h2_b], writes=[h2d_b], stream="h2s")
        if debug:
            final.append(k.dma("gpsimd",
                  dbg_x2.rearrange("(s p) d -> s p d", p=128)[tb], x2_t[:],
                  reads=[x2_b], stream="dbg"))
            final.append(k.dma("gpsimd",
                  dbg_h2.rearrange("(s p) d -> s p d", p=128)[tb], h2_t[:],
                  reads=[h2_b], stream="dbg"))

    p2.close()
    k.fence()
    L["phase3_out"] = dict(x2_dram=x2_dram, x2d_b=x2d_b, h2_dram=h2_dram, h2d_b=h2d_b)


def emit_ffn(nc, k, st, final, debug, L, stop=99):
    sb = L["sb"]
    din, dout = L["din"], L["dout"]
    ps, psb = L["ps"], L["psb"]
    p3 = L["phase3_out"]
    h2_dram, h2d_b = p3["h2_dram"], p3["h2d_b"]
    x2_dram, x2d_b = p3["x2_dram"], p3["x2d_b"]
    from concourse.bass_core_rust import IndirectOffsetOnAxis

    gwT_in = din("gwT", [D, 16])
    wsgT_in = din("wsgT", [D, 2048])
    wsuT_in = din("wsuT", [D, 2048])
    wsdT_in = din("wsdT", [2048, D])
    wgT_in = [din(f"wgT{i}", [D, 2048]) for i in range(4)]
    wuT_in = [din(f"wuT{i}", [D, 2048]) for i in range(4)]
    wdT_in = [din(f"wdT{i}", [2048, D]) for i in range(4)]
    ident_in = din("ident", [128, 128])
    kth4_in = din("kth4", [128, 4], I32)
    iota32r_in = din("iota32r", [128, 32])
    tloc_in = din("tloc", [128, 128])
    tril_in = din("tril", [128, 128], BF16)
    ones1_in = din("ones1", [1, 128])
    ones128_in = din("ones128", [128, 1])
    ones128b_in = din("ones128b", [128, 1], BF16)

    out_final = dout("out", [T, D], F32)
    outf_b = Buf("out")

    ag2s_in = nc.dram_tensor("ag2s_in", [16, T], F32, kind="Internal").ap()
    ag2s_out = nc.dram_tensor("ag2s_out", [8, 16, T], F32, kind="Internal").ap()
    ag2h_out = nc.dram_tensor("ag2h_out", [8, T, D], BF16, kind="Internal").ap()
    p_dram, pd_b = L["p_dram"], L["pd_b"]
    rs_out = nc.dram_tensor("rs_out", [T, D], F32, kind="Internal").ap()
    auxc_dram = [nc.dram_tensor(f"auxc{i}", [2176, 4], F32, kind="Internal").ap()
                 for i in range(4)]
    sh_dram = nc.dram_tensor("sh_scr", [T, D], F32, kind="Internal").ap()
    ag2si_b, ag2so_b = Buf("ag2s_in"), Buf("ag2s_out")
    ag2ho_b = Buf("ag2h_out")
    rso_b = Buf("rs_out")
    auxd_b = Buf("auxc_dram")
    shd_b = Buf("sh_dram")

    G4 = [[0, 1, 2, 3, 4, 5, 6, 7]]
    pf = ExitStack()
    pa = ExitStack()

    ident_t, ident_b = sb(pf, "ident_t", [128, 128], F32)
    ones1_t, ones1_b = sb(pf, "ones1_t", [1, 128], F32)
    ones128b_t, ones128b_b = sb(pf, "ones128b_t", [128, 1], BF16)
    tril_t, tril_b = sb(pf, "tril_t", [128, 128], BF16)
    kth_t, kth_b = sb(pf, "kth_t", [128, 4, 128], F32)
    nxt4_t, nxt4_b = sb(pf, "nxt4", [128, 4], F32)
    iota32r_t, iota32r_b = sb(pf, "iota32r_t", [128, 32], F32)
    tloc_t, tloc_b = sb(pf, "tloc_t", [128, 128], F32)
    k.dma("sync", ident_t[:], ident_in, writes=[ident_b], stream="w0")
    k.dma("sync", ones1_t[:], ones1_in, writes=[ones1_b], stream="w0")
    k.dma("sync", ones128b_t[:], ones128b_in, writes=[ones128b_b], stream="w0")
    k.dma("sync", tril_t[:], tril_in, writes=[tril_b], stream="w0")
    k.dma("sync", iota32r_t[:], iota32r_in, writes=[iota32r_b], stream="w0")
    k.dma("sync", tloc_t[:], tloc_in, writes=[tloc_b], stream="w0")

    k.mark("gateA")
    gw_t, gw_b = sb(pa, "gw_t", [128, 8, 16], BF16)
    scomb, scomb_b = sb(pa, "scomb", [128, 32, 16], F32)
    red_t, red_b = sb(pa, "red_t", [128, 4, 32], F32)
    sct_t, sct_b = sb(pa, "sct", [16, 32, 128], F32)
    h2row, h2row_b = sb(pa, "h2row", [128, 2, D], BF16)
    h2ft, h2ft_b = sb(pa, "h2ft", [128, 2, 8, 128], BF16, is_transpose_dest=True)

    k.dma("gpsimd", gw_t[:], gwT_in.rearrange("(c p) e -> p c e", p=128),
          writes=[gw_b], stream="gw")

    for tb in range(32):
        k.dma("sync", h2row[:, tb % 2, :],
              h2_dram.rearrange("(s p) d -> s p d", p=128)[tb],
              reads=[h2d_b], writes=[h2row_b], stream="xh")
        k.dma("sync", h2ft[:, tb % 2, :, :],
              h2row[:, tb % 2, :].rearrange("p (c n) -> p c n", n=128),
              reads=[h2row_b], writes=[h2ft_b], stream="tr", transpose=True)

        def mm_gate(t, tb=tb):
            last = None
            for kk in range(8):
                last = t.matmul(ps[0][:, ts(tb, 16)], h2ft[:, tb % 2, kk, :],
                                gw_t[:, kk, :], start=(kk == 0), stop=(kk == 7))
            return last

        k.op("tensor", mm_gate, reads=[h2ft_b, gw_b], writes=[psb[0]])
    k.op("vector", lambda v: v.tensor_copy(scomb[:].rearrange("p a e -> p (a e)"),
                                           ps[0][:]),
         reads=[psb[0]], writes=[scomb_b])
    k.op("vector", lambda v: v.tensor_reduce(red_t[:, 0, :, None], scomb[:], axis=AX.X,
                                             op=ALU.max),
         reads=[scomb_b], writes=[red_b])
    k.op("vector", lambda v: v.scalar_tensor_tensor(
             scomb[:], scomb[:], 1.0, red_t[:, 0, :, None].broadcast_to([128, 32, 16]),
             op0=ALU.mult, op1=ALU.subtract),
         reads=[scomb_b, red_b], writes=[scomb_b])
    k.op("scalar", lambda s: s.activation(scomb[:].rearrange("p a e -> p (a e)"),
                                          scomb[:].rearrange("p a e -> p (a e)"), AF.Exp),
         reads=[scomb_b], writes=[scomb_b])
    k.op("vector", lambda v: v.tensor_reduce(red_t[:, 1, :, None], scomb[:], axis=AX.X,
                                             op=ALU.add),
         reads=[scomb_b], writes=[red_b])
    k.op("vector", lambda v: v.reciprocal(red_t[:, 2, :], red_t[:, 1, :]),
         reads=[red_b], writes=[red_b])
    k.op("vector", lambda v: v.tensor_mul(scomb[:], scomb[:],
                                          red_t[:, 2, :, None].broadcast_to([128, 32, 16])),
         reads=[scomb_b, red_b], writes=[scomb_b])

    for rr in range(8):
        def mm_tr(t, rr=rr):
            last = None
            for i in range(4):
                last = t.transpose(ps[1][0:16, ts(i, 128)], scomb[:, rr * 4 + i, :],
                                   ident_t[:])
            return last

        k.op("tensor", mm_tr, reads=[scomb_b, ident_b], writes=[psb[1]])
        k.op("vector", lambda v, rr=rr:
             v.tensor_copy(sct_t[0:16, ts(rr, 4), :],
                           ps[1][0:16, :].rearrange("p (a n) -> p a n", n=128)),
             reads=[psb[1]], writes=[sct_b])
    k.dma("sync", ag2s_in, sct_t[:].rearrange("p a n -> p (a n)"),
          reads=[sct_b], writes=[ag2si_b], stream="ag")
    k.op("gpsimd", lambda g: g.collective_compute(
              "AllGather", ALU.bypass, ins=[ag2s_in], outs=[ag2s_out],
              replica_groups=G4),
          reads=[ag2si_b], writes=[ag2so_b])
    k.op("gpsimd", lambda g: g.collective_compute(
              "AllGather", ALU.bypass, ins=[h2_dram], outs=[ag2h_out],
              replica_groups=G4),
          reads=[h2d_b], writes=[ag2ho_b],
         extra=[Tok(k._stream("h2s"), 0, "h2s")])
    if stop <= 1:
        pa.close()
        pf.close()
        return

    k.mark("bisect")
    kidx_t, kidx_b = sb(pa, "kidx", [128, 4], I32)
    tau_t, tau_b = sb(pa, "tau", [1, 8], F32)
    ones128_t, ones128_b = sb(pa, "ones128_t", [128, 1], F32)
    bis_t, bis_b = sb(pa, "bis_t", [1, 4, 4], F32)
    redc_t, redc_b = sb(pa, "redc_t", [128, 4], F32)
    cmps_t, cmps_b = sb(pa, "cmps_t", [128, 128], F32)
    mk_t, mk_b = sb(pa, "mk_t", [1, 4, 2], U32)
    k.dma("sync", ones128_t[:], ones128_in, writes=[ones128_b], stream="w0")
    k.dma("sync", kidx_t[:], kth4_in, writes=[kidx_b], stream="w0")
    for pl in range(4):
        k.dop("gpsimd", lambda g, pl=pl: g.indirect_dma_start(
                  kth_t[:, pl, :], None,
                  ag2s_out.rearrange("c e (s n) -> (c e s) n", n=128),
                  IndirectOffsetOnAxis(kidx_t[:, pl:pl + 1], 0)),
              reads=[kidx_b, ag2so_b], writes=[kth_b], stream="gg")

    LO = lambda: bis_t[0:1, :, 0]
    HI = lambda: bis_t[0:1, :, 1]
    MID = lambda: bis_t[0:1, :, 2]
    NXT = lambda: bis_t[0:1, :, 3]
    k.op("vector", lambda v: v.memset(bis_t[0:1, :, 0], 0.0), writes=[bis_b])
    k.op("vector", lambda v: v.memset(bis_t[0:1, :, 1], 1.0), writes=[bis_b])
    k.op("vector", lambda v: v.memset(bis_t[0:1, :, 2], 0.5), writes=[bis_b])

    def count_round(src_fn):
        k.op("tensor", lambda t: t.matmul(ps[2][:, 0:4], ones1_t[:], src_fn(),
                                          start=True, stop=True),
             reads=[bis_b, ones1_b], writes=[psb[2]])
        for pl in range(4):
            k.op("vector", lambda v, pl=pl: v.tensor_scalar(
                     cmps_t[:], kth_t[:, pl, :], ps[2][:, pl:pl + 1], None,
                     op0=ALU.is_gt, op1=ALU.add, accum_out=redc_t[:, pl:pl + 1]),
                 reads=[kth_b, psb[2]], writes=[cmps_b, redc_b])
        k.op("tensor", lambda t: t.matmul(ps[3][0:1, 0:4], ones128_t[:], redc_t[:],
                                          start=True, stop=True),
             reads=[redc_b, ones128_b], writes=[psb[3]])

    for it in range(34):
        count_round(MID)
        k.op("vector", lambda v: v.tensor_scalar(mk_t[0:1, :, 0], ps[3][0:1, 0:4],
                                                 float(CAP), None, op0=ALU.is_ge),
             reads=[psb[3]], writes=[mk_b])
        k.op("vector", lambda v: v.tensor_scalar(mk_t[0:1, :, 1], ps[3][0:1, 0:4],
                                                 float(CAP), None, op0=ALU.is_lt),
             reads=[psb[3]], writes=[mk_b])
        k.op("vector", lambda v: v.copy_predicated(LO(), mk_t[0:1, :, 0], MID()),
             reads=[mk_b, bis_b], writes=[bis_b])
        k.op("vector", lambda v: v.copy_predicated(HI(), mk_t[0:1, :, 1], MID()),
             reads=[mk_b, bis_b], writes=[bis_b])
        k.op("vector", lambda v: v.tensor_add(MID(), LO(), HI()),
             reads=[bis_b], writes=[bis_b])
        k.op("vector", lambda v: v.tensor_scalar_mul(MID(), MID(), 0.5),
             reads=[bis_b], writes=[bis_b])
    count_round(LO)
    k.op("vector", lambda v: v.tensor_scalar(mk_t[0:1, :, 0], ps[3][0:1, 0:4],
                                             float(CAP), None, op0=ALU.is_equal),
         reads=[psb[3]], writes=[mk_b])
    k.op("vector", lambda v: v.tensor_copy(NXT(), HI()), reads=[bis_b], writes=[bis_b])
    k.op("vector", lambda v: v.copy_predicated(NXT(), mk_t[0:1, :, 0], LO()),
         reads=[mk_b, bis_b], writes=[bis_b])
    k.op("vector", lambda v: v.tensor_copy(tau_t[0:1, 0:4], NXT()),
         reads=[bis_b], writes=[tau_b])
    k.op("tensor", lambda t: t.matmul(ps[2][:, 40:44], ones1_t[:], tau_t[0:1, 0:4],
                                      start=True, stop=True),
         reads=[tau_b, ones1_b], writes=[psb[2]])
    k.op("vector", lambda v: v.tensor_copy(nxt4_t[:], ps[2][:, 40:44]),
         reads=[psb[2]], writes=[nxt4_b])
    if debug:
        dbg_tau = dout("dbg_tau", [1, 8], F32)
        final.append(k.dma("gpsimd", dbg_tau, tau_t[:], reads=[tau_b], stream="dbg"))
    if stop <= 2:
        pa.close()
        pf.close()
        return

    pa.close()
    k.fence()
    EX = dict(
        ident_t=ident_t, ident_b=ident_b, ones1_t=ones1_t, ones1_b=ones1_b,
        ones128b_t=ones128b_t, ones128b_b=ones128b_b, tril_t=tril_t, tril_b=tril_b,
        kth_t=kth_t, kth_b=kth_b, nxt4_t=nxt4_t, nxt4_b=nxt4_b,
        iota32r_t=iota32r_t,
        iota32r_b=iota32r_b, tloc_t=tloc_t, tloc_b=tloc_b,
        wsgT_in=wsgT_in, wsuT_in=wsuT_in, wsdT_in=wsdT_in,
        wgT_in=wgT_in, wuT_in=wuT_in, wdT_in=wdT_in,
        ag2h_out=ag2h_out, ag2ho_b=ag2ho_b,
        p_dram=p_dram, pd_b=pd_b, rs_out=rs_out, rso_b=rso_b,
        auxc_dram=auxc_dram, auxd_b=auxd_b, sh_dram=sh_dram, shd_b=shd_b,
        out_final=out_final, outf_b=outf_b, G4=G4, IOA=IndirectOffsetOnAxis,
        RB={},
    )
    emit_experts(nc, k, st, final, debug, L, EX)
    if stop <= 5:
        pf.close()
        return
    emit_shared(nc, k, st, final, debug, L, EX)
    if stop <= 6:
        pf.close()
        return
    emit_final(nc, k, st, final, debug, L, EX)
    pf.close()


def emit_shared(nc, k, st, final, debug, L, EX):
    k.mark("shared")
    sb = L["sb"]
    ps, psb = L["ps"], L["psb"]
    p3 = L["phase3_out"]
    h2_dram, h2d_b = p3["h2_dram"], p3["h2d_b"]
    sh_dram, shd_b = EX["sh_dram"], EX["shd_b"]

    pe = ExitStack()
    wsg_t, wsg_b = sb(pe, "wsg_t", [128, 8, 2048], BF16)
    wsu_t, wsu_b = sb(pe, "wsu_t", [128, 8, 2048], BF16)
    wsd_t, wsd_b = sb(pe, "wsd_t", [128, 16, D], BF16)
    h2r2, h2r2_b = sb(pe, "h2r2", [128, 2, D], BF16)
    h2f2, h2f2_b = sb(pe, "h2f2", [128, 2, 8, 512], BF16, is_transpose_dest=True)
    gsh_t, gsh_b = sb(pe, "gsh", [128, 512], BF16)
    ush_t, ush_b = sb(pe, "ush", [128, 512], BF16)
    hsh_t, hsh_b = sb(pe, "hsh", [128, 16, 512], BF16)
    shs_t, shs_b = sb(pe, "shs", [128, 2, D], F32)

    k.dma("gpsimd", wsg_t[:], EX["wsgT_in"].rearrange("(c p) f -> p c f", p=128),
          writes=[wsg_b], stream="gw")
    k.dma("gpsimd", wsu_t[:], EX["wsuT_in"].rearrange("(c p) f -> p c f", p=128),
          writes=[wsu_b], stream="gw")
    k.dma("gpsimd", wsd_t[:], EX["wsdT_in"].rearrange("(c p) d -> p c d", p=128),
          writes=[wsd_b], stream="gw")
    k.op("gpsimd", lambda g: g.collective_compute(
             "ReduceScatter", ALU.add, ins=[EX["p_dram"]], outs=[EX["rs_out"]],
             replica_groups=EX["G4"]),
         reads=[EX["pd_b"]], writes=[EX["rso_b"]])

    def load_tile(tt):
        for rb in range(4):
            tb = tt * 4 + rb
            k.dma("sync", h2r2[:, tb % 2, :],
                  h2_dram.rearrange("(s p) d -> s p d", p=128)[tb],
                  reads=[h2d_b], writes=[h2r2_b], stream="xh")
            k.dma("sync", h2f2[:, tt % 2, :, ts(rb, 128)],
                  h2r2[:, tb % 2, :].rearrange("p (c n) -> p c n", n=128),
                  reads=[h2r2_b], writes=[h2f2_b], stream="tr", transpose=True)

    load_tile(0)
    for tt in range(8):
        for j in range(16):
            pj = j % 2

            def mm_g(t, tt=tt, j=j, pj=pj):
                last = None
                for kk in range(8):
                    last = t.matmul(ps[pj][:, :], wsg_t[:, kk, ts(j, 128)],
                                    h2f2[:, tt % 2, kk, :],
                                    start=(kk == 0), stop=(kk == 7))
                return last

            k.op("tensor", mm_g, reads=[wsg_b, h2f2_b], writes=[psb[pj]])
            k.op("scalar", lambda s, pj=pj: s.activation(gsh_t[:], ps[pj][:], AF.Gelu),
                 reads=[psb[pj]], writes=[gsh_b])

            def mm_u(t, tt=tt, j=j, pj=pj):
                last = None
                for kk in range(8):
                    last = t.matmul(ps[2 + pj][:, :], wsu_t[:, kk, ts(j, 128)],
                                    h2f2[:, tt % 2, kk, :],
                                    start=(kk == 0), stop=(kk == 7))
                return last

            k.op("tensor", mm_u, reads=[wsu_b, h2f2_b], writes=[psb[2 + pj]])
            k.op("scalar", lambda s, pj=pj: s.copy(ush_t[:], ps[2 + pj][:]),
                 reads=[psb[2 + pj]], writes=[ush_b])
            k.op("vector", lambda v, j=j: v.tensor_mul(hsh_t[:, j, :], gsh_t[:],
                                                       ush_t[:]),
                 reads=[gsh_b, ush_b], writes=[hsh_b])
        if tt + 1 < 8:
            load_tile(tt + 1)
        for rb in range(4):
            tb = tt * 4 + rb
            for half in range(2):
                bank = 4 + half

                def mm_dn(t, rb=rb, half=half, bank=bank):
                    last = None
                    for j in range(16):
                        last = t.matmul(ps[bank][:, :], hsh_t[:, j, ts(rb, 128)],
                                        wsd_t[:, j, ts(half, 512)],
                                        start=(j == 0), stop=(j == 15))
                    return last

                k.op("tensor", mm_dn, reads=[hsh_b, wsd_b], writes=[psb[bank]])
                k.op("vector", lambda v, tb=tb, half=half, bank=bank:
                     v.tensor_copy(shs_t[:, tb % 2, ts(half, 512)], ps[bank][:]),
                     reads=[psb[bank]], writes=[shs_b])
            k.dma("sync", sh_dram.rearrange("(s p) d -> s p d", p=128)[tb],
                  shs_t[:, tb % 2, :], reads=[shs_b], writes=[shd_b], stream="shs")
    if debug:
        dout = L["dout"]
        dbg_sh = dout("dbg_sh", [T, D], F32)
        final.append(k.dma("gpsimd", dbg_sh, sh_dram, reads=[shd_b], stream="dbg"))
    pe.close()
    k.fence()


def emit_experts(nc, k, st, final, debug, L, EX):
    k.mark("experts")
    sb = L["sb"]
    ps, psb = L["ps"], L["psb"]
    dout = L["dout"]
    IOA = EX["IOA"]
    kth_t, kth_b = EX["kth_t"], EX["kth_b"]
    nxt4_t, nxt4_b = EX["nxt4_t"], EX["nxt4_b"]
    ident_t, ident_b = EX["ident_t"], EX["ident_b"]
    ones1_t, ones1_b = EX["ones1_t"], EX["ones1_b"]
    ones128b_t, ones128b_b = EX["ones128b_t"], EX["ones128b_b"]
    tril_t, tril_b = EX["tril_t"], EX["tril_b"]
    iota32r_t, iota32r_b = EX["iota32r_t"], EX["iota32r_b"]
    tloc_t, tloc_b = EX["tloc_t"], EX["tloc_b"]
    ag2h_out, ag2ho_b = EX["ag2h_out"], EX["ag2ho_b"]
    p_dram, pd_b = EX["p_dram"], EX["pd_b"]
    auxc_dram, auxd_b = EX["auxc_dram"], EX["auxd_b"]
    wgT_in, wuT_in, wdT_in = EX["wgT_in"], EX["wuT_in"], EX["wdT_in"]
    G4 = EX["G4"]
    RB = EX["RB"]

    pe = ExitStack()
    sc_std, scstd_b = sb(pe, "sc_std", [128, 128], F32)
    mpf_t, mpf_b = sb(pe, "mpf", [128, 128], F32)
    mpb_t, mpb_b = sb(pe, "mpb", [128, 128], BF16)
    p1sb_t, p1sb_b = sb(pe, "p1sb", [128, 128], F32)
    csr2_t, csr2_b = sb(pe, "csr2", [1, 256], F32)
    grank_t, grank_b = sb(pe, "grank", [128, 128], F32)
    scat_t, scat_b = sb(pe, "scat", [128, 128], F32)
    lsel_t, lsel_b = sb(pe, "lsel", [128, 128], F32)
    self_f, selff_b = sb(pe, "self_f", [128, 4, 32], F32)
    aux4_t, aux4_b = sb(pe, "aux4", [128, 4, 4], F32)
    auxf_t, auxf_b = sb(pe, "auxf", [128, 4], F32)
    scati_t, scati_b = sb(pe, "scati", [128, 1], I32)
    auxs_t, auxs_b = sb(pe, "auxs", [128, 17, 4], F32)
    toki_t, toki_b = sb(pe, "toki", [128, 17], I32)
    xcb_t, xcb_b = sb(pe, "xcb", [128, 2, D], BF16)
    xf_t, xf_b = sb(pe, "xf", [128, 8, CAP], BF16, is_transpose_dest=True)
    wg_t, wg_b = sb(pe, "wg_t", [128, 8, 2048], BF16)
    wu_t, wu_b = sb(pe, "wu_t", [128, 8, 2048], BF16)
    wd_t, wd_b = sb(pe, "wd_t", [128, 16, D], BF16)
    gg_t, gg_b = sb(pe, "gg", [128, 512], BF16)
    uu_t, uu_b = sb(pe, "uu", [128, 512], BF16)
    hbuf_t, hbuf_b = sb(pe, "hbuf", [128, 16, 512], BF16)
    eos_t, eos_b = sb(pe, "eos", [128, 2, D], F32)
    eosb = [Buf("eos0"), Buf("eos1")]
    zz_t, zz_b = sb(pe, "zz", [128, 68], F32)

    k.op("vector", lambda v: v.memzero(zz_t[:]), writes=[zz_b])
    for q in range(4):
        k.dma("sync", auxc_dram[q].rearrange("(a p) w -> p a w", p=128),
              zz_t[:].rearrange("p (a w) -> p a w", w=4),
              reads=[zz_b], writes=[auxd_b], stream="xcw")

    def load_w(el):
        k.dma("gpsimd", wg_t[:], wgT_in[el].rearrange("(c p) f -> p c f", p=128),
              writes=[wg_b], stream="gw")
        k.dma("gpsimd", wu_t[:], wuT_in[el].rearrange("(c p) f -> p c f", p=128),
              writes=[wu_b], stream="gw")
        k.dma("gpsimd", wd_t[:], wdT_in[el].rearrange("(c p) d -> p c d", p=128),
              writes=[wd_b], stream="gw")

    load_w(0)
    OOB = 3000.0
    for el in range(4):
        k.mark(f"prep{el}")
        k.op("tensor", lambda t, el=el: t.transpose(ps[4][:, 0:128],
                                                    kth_t[:, el, :], ident_t[:]),
             reads=[kth_b, ident_b], writes=[psb[4]])
        k.op("vector", lambda v: v.tensor_copy(sc_std[:], ps[4][:, 0:128]),
             reads=[psb[4]], writes=[scstd_b])
        k.op("vector", lambda v, el=el: v.tensor_scalar(
                 mpf_t[:], sc_std[:], nxt4_t[:, el:el + 1], None, op0=ALU.is_gt),
             reads=[scstd_b, nxt4_b], writes=[mpf_b])
        k.op("vector", lambda v: v.tensor_copy(mpb_t[:], mpf_t[:]),
             reads=[mpf_b], writes=[mpb_b])
        k.op("tensor", lambda t: t.matmul(ps[4][:, 0:128], tril_t[:], mpb_t[:],
                                          start=True, stop=True),
             reads=[tril_b, mpb_b], writes=[psb[4]])
        k.op("tensor", lambda t: t.matmul(ps[5][0:1, 0:128], ones128b_t[:],
                                          mpb_t[:], start=True, stop=True),
             reads=[ones128b_b, mpb_b], writes=[psb[5]])
        k.op("vector", lambda v: v.tensor_copy(csr2_t[0:1, 0:128], ps[5][0:1, 0:128]),
             reads=[psb[5]], writes=[csr2_b])
        for sh in (1, 2, 4, 8, 16, 32, 64):
            k.op("vector", lambda v, sh=sh: v.tensor_add(
                     csr2_t[0:1, sh:128], csr2_t[0:1, sh:128], csr2_t[0:1, 0:128 - sh]),
                 reads=[csr2_b], writes=[csr2_b])
        k.op("vector", lambda v: v.tensor_sub(csr2_t[0:1, 128:256], csr2_t[0:1, 0:128],
                                              ps[5][0:1, 0:128]),
             reads=[csr2_b, psb[5]], writes=[csr2_b])
        k.op("tensor", lambda t: t.matmul(ps[6][:, 0:128], ones1_t[:],
                                          csr2_t[0:1, 128:256], start=True, stop=True),
             reads=[csr2_b, ones1_b], writes=[psb[6]])
        k.op("vector", lambda v: v.tensor_copy(p1sb_t[:], ps[4][:, 0:128]),
             reads=[psb[4]], writes=[p1sb_b])
        k.op("vector", lambda v: v.tensor_add(grank_t[:], p1sb_t[:], ps[6][:, 0:128]),
             reads=[p1sb_b, psb[6]], writes=[grank_b])
        k.op("vector", lambda v: v.tensor_scalar(scat_t[:], grank_t[:], -OOB,
                                                 None, op0=ALU.add),
             reads=[grank_b], writes=[scat_b])
        k.op("vector", lambda v: v.tensor_mul(scat_t[:], scat_t[:], mpf_t[:]),
             reads=[scat_b, mpf_b], writes=[scat_b])
        k.op("vector", lambda v: v.tensor_scalar(scat_t[:], scat_t[:], OOB,
                                                 None, op0=ALU.add),
             reads=[scat_b], writes=[scat_b])
        k.op("vector", lambda v: v.tensor_scalar(lsel_t[:], p1sb_t[:], 5.0, None,
                                                 op0=ALU.add),
             reads=[p1sb_b], writes=[lsel_b])
        k.op("vector", lambda v: v.tensor_mul(lsel_t[:], lsel_t[:], mpf_t[:]),
             reads=[lsel_b, mpf_b], writes=[lsel_b])
        k.op("vector", lambda v: v.tensor_scalar(lsel_t[:], lsel_t[:], -5.0, None,
                                                 op0=ALU.add),
             reads=[lsel_b], writes=[lsel_b])

        k.mark(f"pack{el}")
        for grp in range(32):
            k.op("vector", lambda v, grp=grp: v.tensor_tensor(
                     self_f[:], iota32r_t[:, None, :].broadcast_to([128, 4, 32]),
                     lsel_t[:, ts(grp, 4)][:, :, None].broadcast_to([128, 4, 32]),
                     op=ALU.is_equal),
                 reads=[iota32r_b, lsel_b], writes=[selff_b])
            k.op("vector", lambda v, grp=grp: v.tensor_copy(
                     aux4_t[:, :, 0], scat_t[:, ts(grp, 4)]),
                 reads=[scat_b], writes=[aux4_b])
            k.op("vector", lambda v, grp=grp: v.tensor_copy(
                     aux4_t[:, :, 1], tloc_t[:, ts(grp, 4)]),
                 reads=[tloc_b], writes=[aux4_b])
            k.op("vector", lambda v, grp=grp: v.tensor_copy(
                     aux4_t[:, :, 2], sc_std[:, ts(grp, 4)]),
                 reads=[scstd_b], writes=[aux4_b])
            k.op("vector", lambda v, grp=grp: v.tensor_copy(
                     aux4_t[:, :, 3], mpf_t[:, ts(grp, 4)]),
                 reads=[mpf_b], writes=[aux4_b])
            k.op("tensor", lambda t: t.matmul(
                     ps[2][:, 0:16], self_f[:].rearrange("p a b -> p (a b)"),
                     aux4_t[:].rearrange("p a b -> p (a b)"), start=True, stop=True),
                 reads=[selff_b, aux4_b], writes=[psb[2]])
            for i in range(4):
                k.op("vector", lambda v, i=i: v.tensor_copy(
                         auxf_t[32 * i:32 * i + 32, :],
                         ps[2][32 * i:32 * i + 32, 4 * i:4 * i + 4]),
                     reads=[psb[2]], writes=[auxf_b])
            k.op("vector", lambda v: v.scalar_tensor_tensor(
                     auxf_t[:, 0:1], auxf_t[:, 3:4], -OOB, auxf_t[:, 0:1],
                     op0=ALU.mult, op1=ALU.add),
                 reads=[auxf_b], writes=[auxf_b])
            k.op("vector", lambda v: v.tensor_scalar(auxf_t[:, 0:1], auxf_t[:, 0:1],
                                                     OOB, None, op0=ALU.add),
                 reads=[auxf_b], writes=[auxf_b])
            k.op("vector", lambda v: v.tensor_copy(scati_t[:], auxf_t[:, 0:1]),
                 reads=[auxf_b], writes=[scati_b])
            k.dop("gpsimd", lambda g, el=el: g.indirect_dma_start(
                      auxc_dram[el], IOA(scati_t[:], 0), auxf_t[:], None,
                      bounds_check=breg(g, RB, CAP - 1), oob_is_err=False),
                  reads=[scati_b, auxf_b], writes=[auxd_b], stream="xcw")

        k.mark(f"gath{el}")
        k.dma("sync", auxs_t[:], auxc_dram[el].rearrange("(a p) w -> p a w", p=128),
              reads=[auxd_b], writes=[auxs_b], stream="xcr")
        k.op("vector", lambda v: v.tensor_copy(toki_t[:], auxs_t[:, :, 1]),
             reads=[auxs_b], writes=[toki_b])
        for blk in range(16):
            k.dop("gpsimd", lambda g, blk=blk: g.indirect_dma_start(
                      xcb_t[:, blk % 2, :], None,
                      ag2h_out.rearrange("c t d -> (c t) d"),
                      IOA(toki_t[:, blk:blk + 1], 0),
                      bounds_check=breg(g, RB, 8 * T - 1), oob_is_err=False),
                  reads=[toki_b, ag2ho_b], writes=[xcb_b], stream="gg")
            k.dma("sync", xf_t[:, :, ts(blk, 128)],
                  xcb_t[:, blk % 2, :].rearrange("p (c n) -> p c n", n=128),
                  reads=[xcb_b], writes=[xf_b], stream="tr", transpose=True)

        k.mark(f"ffn{el}")
        for sup in range(4):
            for j in range(16):
                pj = j % 2

                def mm_gx(t, sup=sup, j=j, pj=pj):
                    last = None
                    for kk in range(8):
                        last = t.matmul(ps[pj][:, :], wg_t[:, kk, ts(j, 128)],
                                        xf_t[:, kk, ts(sup, 512)],
                                        start=(kk == 0), stop=(kk == 7))
                    return last

                k.op("tensor", mm_gx, reads=[wg_b, xf_b], writes=[psb[pj]])
                k.op("scalar", lambda s, pj=pj: s.activation(gg_t[:], ps[pj][:],
                                                             AF.Gelu),
                     reads=[psb[pj]], writes=[gg_b])

                def mm_ux(t, sup=sup, j=j, pj=pj):
                    last = None
                    for kk in range(8):
                        last = t.matmul(ps[2 + pj][:, :], wu_t[:, kk, ts(j, 128)],
                                        xf_t[:, kk, ts(sup, 512)],
                                        start=(kk == 0), stop=(kk == 7))
                    return last

                k.op("tensor", mm_ux, reads=[wu_b, xf_b], writes=[psb[2 + pj]])
                k.op("scalar", lambda s, pj=pj: s.copy(uu_t[:], ps[2 + pj][:]),
                     reads=[psb[2 + pj]], writes=[uu_b])
                k.op("vector", lambda v, j=j: v.tensor_mul(hbuf_t[:, j, :], gg_t[:],
                                                           uu_t[:]),
                     reads=[gg_b, uu_b], writes=[hbuf_b])
            if sup == 3 and el + 1 < 4:
                k.dma("gpsimd", wg_t[:],
                      wgT_in[el + 1].rearrange("(c p) f -> p c f", p=128),
                      writes=[wg_b], stream="gw")
                k.dma("gpsimd", wu_t[:],
                      wuT_in[el + 1].rearrange("(c p) f -> p c f", p=128),
                      writes=[wu_b], stream="gw")
            for sbk in range(4):
                for half in range(2):
                    bank = 4 + half

                    def mm_dn(t, sbk=sbk, half=half, bank=bank):
                        last = None
                        for j in range(16):
                            last = t.matmul(ps[bank][:, :],
                                            hbuf_t[:, j, ts(sbk, 128)],
                                            wd_t[:, j, ts(half, 512)],
                                            start=(j == 0), stop=(j == 15))
                        return last

                    k.op("tensor", mm_dn, reads=[hbuf_b, wd_b], writes=[psb[bank]])
                    k.op("vector", lambda v, sup=sup, sbk=sbk, half=half, bank=bank:
                         v.tensor_scalar(eos_t[:, (sup * 4 + sbk) % 2, ts(half, 512)],
                                         ps[bank][:],
                                         auxs_t[:, sup * 4 + sbk, 2:3], None,
                                         op0=ALU.mult),
                         reads=[psb[bank], auxs_b],
                         writes=[eosb[(sup * 4 + sbk) % 2]])
                blk = sup * 4 + sbk
                k.dop("gpsimd", lambda g, blk=blk: g.indirect_dma_start(
                          p_dram, IOA(toki_t[:, blk:blk + 1], 0),
                          eos_t[:, blk % 2, :], None,
                          bounds_check=breg(g, RB, 8 * T - 1), oob_is_err=False,
                          compute_op=ALU.add),
                      reads=[eosb[blk % 2], toki_b], writes=[pd_b], stream="gg")
        if el + 1 < 4:
            k.dma("gpsimd", wd_t[:],
                  wdT_in[el + 1].rearrange("(c p) d -> p c d", p=128),
                  writes=[wd_b], stream="gw")

    if debug:
        dbg_aux = dout("dbg_aux", [128, 17, 4], F32)
        final.append(k.dma("gpsimd", dbg_aux, auxs_t[:], reads=[auxs_b], stream="dbg"))
    pe.close()
    k.fence()


def emit_final(nc, k, st, final, debug, L, EX):
    k.mark("final")
    sb = L["sb"]
    ps, psb = L["ps"], L["psb"]
    p3 = L["phase3_out"]
    x2_dram, x2d_b = p3["x2_dram"], p3["x2d_b"]
    sh_dram, shd_b = EX["sh_dram"], EX["shd_b"]
    p_dram, pd_b = EX["p_dram"], EX["pd_b"]
    rs_out, rso_b = EX["rs_out"], EX["rso_b"]
    out_final, outf_b = EX["out_final"], EX["outf_b"]
    G4 = EX["G4"]

    pe = ExitStack()
    W = 4
    x2r_t, x2r_b = sb(pe, "x2r", [128, W, D], F32)
    shr_t, shr_b = sb(pe, "shr", [128, W, D], F32)
    rsr_t, rsr_b = sb(pe, "rsr", [128, W, D], F32)
    ot, ot_b = [], []
    for i in range(W):
        t, b = sb(pe, f"otile{i}", [128, D], F32)
        ot.append(t)
        ot_b.append(b)

    for tb in range(32):
        sl = tb % W
        k.dma("sync", x2r_t[:, sl, :],
              x2_dram.rearrange("(s p) d -> s p d", p=128)[tb],
              reads=[x2d_b], writes=[x2r_b], stream="x2l")
        k.dma("sync", shr_t[:, sl, :],
              sh_dram.rearrange("(s p) d -> s p d", p=128)[tb],
              reads=[shd_b], writes=[shr_b], stream="shl")
        k.dma("sync", rsr_t[:, sl, :],
              rs_out.rearrange("(s p) d -> s p d", p=128)[tb],
              reads=[rso_b], writes=[rsr_b], stream="rsl")
        k.op("vector", lambda v, sl=sl: v.tensor_add(
                 ot[sl][:], x2r_t[:, sl, :], shr_t[:, sl, :]),
             reads=[x2r_b, shr_b], writes=[ot_b[sl]])
        k.op("vector", lambda v, sl=sl: v.tensor_add(
                 ot[sl][:], ot[sl][:], rsr_t[:, sl, :]),
             reads=[rsr_b, ot_b[sl]], writes=[ot_b[sl]])
        final.append(k.dma("sync",
                           out_final.rearrange("(s p) d -> s p d", p=128)[tb],
                           ot[sl][:], reads=[ot_b[sl]], writes=[outf_b],
                           stream="out"))
    if L.get("probe"):
        dout = L["dout"]
        pr_x2 = dout("pr_x2", [T, D], F32)
        pr_sh = dout("pr_sh", [T, D], F32)
        pr_h2 = dout("pr_h2", [T, D], BF16)
        pr_rs = dout("pr_rs", [256, D], F32)
        final.append(k.dma("gpsimd", pr_x2, x2_dram, reads=[x2d_b], stream="dbg"))
        final.append(k.dma("gpsimd", pr_sh, sh_dram, reads=[shd_b], stream="dbg"))
        final.append(k.dma("gpsimd", pr_h2, p3["h2_dram"], reads=[p3["h2d_b"]], stream="dbg"))
        final.append(k.dma("gpsimd", pr_rs, rs_out[0:256, :], reads=[rso_b], stream="dbg"))
    pe.close()



def _consts():
    epse = np.zeros((128, 512), np.float32)
    for p in range(128):
        for lh in range(8):
            epse[p, lh * 64 + (p % 64)] = 1e-4
    gjm = np.ones((128, 16, 16), np.float32)
    for j in range(16):
        gjm[:, j, j] = 0
    gjm8 = np.ones((128, 8, 8), np.float32)
    for j in range(8):
        gjm8[:, j, j] = 0
    idc = np.zeros((128, 16), np.float32)
    for p in range(128):
        if p % 64 < 16:
            idc[p, p % 64] = 1.0
    BF = ml_dtypes.bfloat16
    ident = np.eye(128, dtype=np.float32)
    iota32r = np.broadcast_to(np.arange(32, dtype=np.float32), (128, 32)).copy()
    tril = np.triu(np.ones((128, 128), np.float32), 1).astype(BF)
    ones1 = np.ones((1, 128), np.float32)
    ones128 = np.ones((128, 1), np.float32)
    ones128b = np.ones((128, 1), BF)
    return dict(epse=epse, gjm=gjm, gjm8=gjm8, idc=idc, ident=ident,
                iota32r=iota32r, tril=tril, ones1=ones1, ones128=ones128,
                ones128b=ones128b)


def _core_consts(c):
    g, r = c // 4, c % 4
    kth4 = np.zeros((128, 4), np.int32)
    for pl in range(4):
        e = 4 * r + pl
        for w in range(128):
            cg, s = w // 32, w % 32
            kth4[w, pl] = ((4 * g + cg) * 16 + e) * 32 + s
    tloc = np.zeros((128, 128), np.float32)
    for w in range(128):
        for p in range(128):
            tloc[p, w] = (4 * g + w // 32) * 4096 + (w % 32) * 128 + p
    cf = np.zeros(8, np.float32)
    for j in range(4):
        if j < r:
            cf[g * 4 + j] = ALPHA ** (32 * (r - 1 - j))
    coef = np.ascontiguousarray(np.broadcast_to(cf, (128, 8)), dtype=np.float32)
    return dict(kth4=kth4, tloc=tloc, coef=coef)


def kernel(**inputs):
    X = np.ascontiguousarray(inputs["L0_attn_x"], dtype=np.float32).reshape(-1, D)
    Wq = np.ascontiguousarray(inputs["L0_attn_Wq"], dtype=np.float32)
    Wk = np.ascontiguousarray(inputs["L0_attn_Wk"], dtype=np.float32)
    Wv = np.ascontiguousarray(inputs["L0_attn_Wv"], dtype=np.float32)
    Wo = np.ascontiguousarray(inputs["L0_attn_Wo"], dtype=np.float32)
    lng = np.ascontiguousarray(
        np.broadcast_to(inputs["L0_attn_ln_g"], (128, D)), dtype=np.float32)
    lnb = np.ascontiguousarray(
        np.broadcast_to(inputs["L0_attn_ln_b"], (128, D)), dtype=np.float32)
    gwT = np.ascontiguousarray(inputs["L0_ffn_gate_w"].T, dtype=np.float32)
    wsgT = np.ascontiguousarray(inputs["L0_ffn_Ws_g"].T, dtype=np.float32)
    wsuT = np.ascontiguousarray(inputs["L0_ffn_Ws_u"].T, dtype=np.float32)
    wsdT = np.ascontiguousarray(inputs["L0_ffn_Ws_d"].T, dtype=np.float32)
    Wg, Wu, Wd = inputs["L0_ffn_Wg"], inputs["L0_ffn_Wu"], inputs["L0_ffn_Wd"]

    nc = build(phase=4, debug=False, ffn_stop=99)
    shared = _consts()
    in_maps = []
    for c in range(8):
        r = c % 4
        im = {"x": np.ascontiguousarray(X[c * T:(c + 1) * T]),
              "wq": Wq, "wk": Wk, "wv": Wv, "wo": Wo, "lng": lng, "lnb": lnb,
              "gwT": gwT, "wsgT": wsgT, "wsuT": wsuT, "wsdT": wsdT}
        im.update(shared)
        im.update(_core_consts(c))
        for pl in range(4):
            e = 4 * r + pl
            im[f"wgT{pl}"] = np.ascontiguousarray(Wg[e].T, dtype=np.float32)
            im[f"wuT{pl}"] = np.ascontiguousarray(Wu[e].T, dtype=np.float32)
            im[f"wdT{pl}"] = np.ascontiguousarray(Wd[e].T, dtype=np.float32)
        in_maps.append(im)
    res = run_bass_kernel_spmd(nc, in_maps, core_ids=list(range(8)))
    out = np.concatenate([np.asarray(res.results[c]["out"], dtype=np.float32)
                          for c in range(8)], axis=0)
    return out.reshape(2, 16384, D)


# version 07 (final)
# speedup vs baseline: 1.1226x; 1.1226x over previous
import numpy as np
import ml_dtypes
import concourse.bass as bass
import concourse.mybir as mybir
from contextlib import ExitStack
from concourse.bass_utils import run_bass_kernel_spmd

F32 = mybir.dt.float32
BF16 = mybir.dt.bfloat16
I16 = mybir.dt.int16
I32 = mybir.dt.int32
U32 = mybir.dt.uint32
AF = mybir.ActivationFunctionType
ALU = mybir.AluOpType
AX = mybir.AxisListType

ENGS = ("tensor", "vector", "scalar", "gpsimd", "sync")

T = 4096
D = 1024
CAP = 2048
NE = 16
EPS_LN = 1e-5
ALPHA = 0.9
SCALE = 0.125

class Tok:
    __slots__ = ("sem", "val", "stream")

    def __init__(s, sem, val, stream=None):
        s.sem, s.val, s.stream = sem, val, stream


class Buf:
    __slots__ = ("w", "readers", "name")

    def __init__(s, name=""):
        s.w, s.readers, s.name = None, [], name


class K:
    def __init__(self, nc):
        self.nc = nc
        self.prog = {e: [] for e in ENGS}
        self.esem = {e: nc.alloc_semaphore(f"es_{e}") for e in ENGS}
        self.ecnt = {e: 0 for e in ENGS}
        self.water = {e: {} for e in ENGS}
        self.dsem = {}
        self.dcnt = {}
        self.marks = [("init", {e: 0 for e in ENGS})]

    def mark(self, name):
        self.marks.append((name, {e: len(self.prog[e]) for e in ENGS}))

    def fence(self):
        toks = [Tok(self.esem[e], self.ecnt[e]) for e in ENGS if self.ecnt[e] > 0]
        toks += [Tok(self.dsem[s], self.dcnt[s]) for s in self.dsem
                 if self.dcnt[s] > 0]
        self.fence_toks = toks

    def _stream(self, name):
        if name not in self.dsem:
            self.dsem[name] = self.nc.alloc_semaphore(f"ds_{name}")
            self.dcnt[name] = 0
        return self.dsem[name]

    def _mkwaits(self, e, deps):
        w = self.water[e]
        need = {}
        for t in deps:
            if t is None:
                continue
            kk = id(t.sem)
            val = self.dcnt[t.stream] if t.stream is not None else t.val
            if w.get(kk, 0) >= val:
                continue
            if need.get(kk, (None, 0))[1] < val:
                need[kk] = (t.sem, val)
        for kk, (sem, v) in need.items():
            w[kk] = v
        return list(need.values())

    def _deps(self, reads, writes):
        deps = []
        for b in reads:
            deps.append(b.w)
        for b in writes:
            deps.append(b.w)
            deps.extend(b.readers)
        return deps

    def _track(self, reads, writes, tok):
        for b in reads:
            b.readers.append(tok)
        for b in writes:
            b.w = tok
            b.readers = []

    def op(self, e, emit, reads=(), writes=(), extra=()):
        deps = self._deps(reads, writes) + list(extra) + \
            getattr(self, "fence_toks", [])
        wl = self._mkwaits(e, deps)
        self.ecnt[e] += 1
        tok = Tok(self.esem[e], self.ecnt[e])

        def clo(eng, wl=wl, emit=emit, tok=tok):
            for (sem, v) in wl:
                eng.wait_ge(sem, v)
            emit(eng).then_inc(tok.sem, 1)

        self.prog[e].append(clo)
        self._track(reads, writes, tok)
        return tok

    def dma(self, e, dst, src, reads=(), writes=(), stream="d0", extra=(), **kw):
        deps = self._deps(reads, writes) + list(extra) + \
            getattr(self, "fence_toks", [])
        wl = self._mkwaits(e, deps)
        sem = self._stream(stream)
        self.dcnt[stream] += 16
        tok = Tok(sem, self.dcnt[stream], stream)

        def clo(eng, wl=wl, dst=dst, src=src, tok=tok, kw=kw):
            for (s, v) in wl:
                eng.wait_ge(s, v)
            eng.dma_start(dst, src, **kw).then_inc(tok.sem, 16)

        self.prog[e].append(clo)
        self._track(reads, writes, tok)
        return tok

    def dop(self, e, emit, reads=(), writes=(), stream="g0", extra=()):
        deps = self._deps(reads, writes) + list(extra) + \
            getattr(self, "fence_toks", [])
        wl = self._mkwaits(e, deps)
        sem = self._stream(stream)
        self.dcnt[stream] += 16
        tok = Tok(sem, self.dcnt[stream], stream)

        def clo(eng, wl=wl, emit=emit, tok=tok):
            for (s, v) in wl:
                eng.wait_ge(s, v)
            emit(eng).then_inc(tok.sem, 16)

        self.prog[e].append(clo)
        self._track(reads, writes, tok)
        return tok

    def finalize(self, final_waits):
        nc = self.nc
        wl = self._mkwaits("sync", list(final_waits))

        def tail(eng, wl=wl):
            for (sem, v) in wl:
                eng.wait_ge(sem, v)

        self.prog["sync"].append(tail)
        marks = self.marks

        def run_eng(eng, e):
            bounds = [m[1][e] for m in marks] + [len(self.prog[e])]
            for i, (name, _) in enumerate(marks):
                a, b = bounds[i], bounds[i + 1]
                if a >= b:
                    continue
                with nc.named_scope(name):
                    for f in self.prog[e][a:b]:
                        f(eng)

        with nc.Block() as blk:
            @blk.tensor
            def _(t):
                run_eng(t, "tensor")

            @blk.vector
            def _(v):
                run_eng(v, "vector")

            @blk.scalar
            def _(s):
                run_eng(s, "scalar")

            @blk.gpsimd
            def _(g):
                run_eng(g, "gpsimd")

            @blk.sync
            def _(s):
                run_eng(s, "sync")


def emit_gj(k, flat_t, flat_b, n, width, scr_t, scr_b, gjmask_t, gjmask_b):
    rcol = scr_t[:, 0:1]
    rowsc = scr_t[:, 1:1 + n]
    colf = scr_t[:, 1 + n:1 + 2 * n]
    upd = scr_t[:, 1 + 2 * n:1 + 2 * n + n * n]
    P = 128
    for w in range(width):
        blk = flat_t[:, w]
        blk3 = flat_t[:, w].rearrange("p (r c) -> p r c", r=n)
        for j in range(n):
            piv = blk[:, j * n + j: j * n + j + 1]
            k.op("vector", lambda v, piv=piv: v.reciprocal(rcol, piv),
                 reads=[flat_b], writes=[scr_b])
            rowj = blk[:, j * n: j * n + n]
            k.op("vector", lambda v, rowj=rowj:
                 v.tensor_scalar(rowsc, rowj, rcol, None, op0=ALU.mult),
                 reads=[flat_b, scr_b], writes=[scr_b])
            colj = blk3[:, :, j]
            k.op("vector", lambda v, colj=colj, j=j:
                 v.scalar_tensor_tensor(colf, colj, -1.0, gjmask_t[:, j],
                                        op0=ALU.mult, op1=ALU.mult),
                 reads=[flat_b, gjmask_b], writes=[scr_b])
            k.op("vector", lambda v:
                 v.tensor_mul(upd.rearrange("p (r c) -> p r c", r=n),
                              colf[:, :, None].broadcast_to([P, n, n]),
                              rowsc[:, None, :].broadcast_to([P, n, n])),
                 reads=[scr_b], writes=[scr_b])
            k.op("vector", lambda v, blk=blk: v.tensor_add(blk, blk, upd),
                 reads=[scr_b], writes=[flat_b])
            k.op("vector", lambda v, rowj=rowj: v.tensor_copy(rowj, rowsc),
                 reads=[scr_b], writes=[flat_b])
            k.op("vector", lambda v, colj=colj:
                 v.tensor_scalar(colj, colf, rcol, None, op0=ALU.mult),
                 reads=[scr_b], writes=[flat_b])
            k.op("vector", lambda v, piv=piv: v.tensor_copy(piv, rcol),
                 reads=[scr_b], writes=[flat_b])



def emit_border_inv(k, G_t, G_b, inv_t, inv_b, t0_t, t0_b, tt_t, tt_b,
                    stg_t, stg_b, is16_t, is16_b, q2_t, q2_b,
                    flat_t, flat_b, flat2_t, flat2_b, scr_t, scr_b,
                    gjm_t, gjm_b, identcol_t, identcol_b,
                    bounce, bounce_b, ps, psb, npair=64):
    NP = npair
    P_ = 128

    def extract(src_fn, read_bufs):
        for h in range(2):
            k.dma("sync", bounce.rearrange("(u h) (r c) -> h r u c", h=2, r=16)[h],
                  src_fn(h), reads=list(read_bufs), writes=[bounce_b], stream="bnc")
        return k.dma("sync", flat_t[:].rearrange("p (r c) -> p r c", r=16),
                     bounce.rearrange("p (r c) -> p r c", r=16),
                     reads=[bounce_b], writes=[flat_b], stream="bnc")

    def extract2(src_fn, read_bufs):
        for h in range(2):
            k.dma("sync", bounce.rearrange("(u h) (r c) -> h r u c", h=2, r=16)[h],
                  src_fn(h), reads=list(read_bufs), writes=[bounce_b], stream="bnc")
        return k.dma("sync", flat2_t[:].rearrange("p (r c) -> p r c", r=16),
                     bounce.rearrange("p (r c) -> p r c", r=16),
                     reads=[bounce_b], writes=[flat2_b], stream="bnc")

    def insert(dst_fns, wbufs):
        k.dma("sync", bounce.rearrange("p (r c) -> p r c", r=16),
              flat_t[:].rearrange("p (r c) -> p r c", r=16),
              reads=[flat_b], writes=[bounce_b], stream="bnc")
        for h in range(2):
            for dst_fn in dst_fns:
                k.dma("sync", dst_fn(h),
                      bounce.rearrange("(u h) (r c) -> h r u c", h=2, r=16)[h],
                      reads=[bounce_b], writes=list(wbufs), stream="bnc")

    def gj16():
        emit_gj(k, flat_t[:].rearrange("p (w f) -> p w f", w=1), flat_b, 16, 1,
                scr_t, scr_b, gjm_t, gjm_b)

    extract(lambda h: G_t[h * 64:h * 64 + 16, :, 0:16], [G_b])
    gj16()
    insert([lambda h: inv_t[h * 64:h * 64 + 16, :, 0:16]], [inv_b])

    for KD in (16, 32, 48):
        KE = KD + 16
        for bank, u0 in ((0, 0), (1, NP // 2)):
            un = NP // 2

            def mm_t0(t, u0=u0, un=un, bank=bank, KD=KD):
                last = None
                for h in range(2):
                    for ui in range(un):
                        u = u0 + ui
                        last = t.matmul(
                            ps[bank][h * 64:h * 64 + KD, ts(ui, 16)],
                            inv_t[h * 64:h * 64 + KD, u, 0:KD],
                            G_t[h * 64:h * 64 + KD, u, KD:KD + 16],
                            start=True, stop=True, tile_position=(h * 64, h * 64))
                return last

            k.op("tensor", mm_t0, reads=[inv_b, G_b], writes=[psb[bank]])
            for h in range(2):
                k.op("vector", lambda v, u0=u0, un=un, bank=bank, h=h, KD=KD:
                     v.tensor_copy(t0_t[h * 64:h * 64 + KD, u0:u0 + un, :],
                                   ps[bank][h * 64:h * 64 + KD].rearrange(
                                       "p (u c) -> p u c", c=16)),
                     reads=[psb[bank]], writes=[t0_b])

        for bank, u0 in ((2, 0), (3, NP // 2)):
            un = NP // 2

            def mm_p(t, u0=u0, un=un, bank=bank, KD=KD):
                last = None
                for h in range(2):
                    for ui in range(un):
                        u = u0 + ui
                        last = t.matmul(
                            ps[bank][h * 64:h * 64 + 16, ts(ui, 16)],
                            G_t[h * 64:h * 64 + KD, u, KD:KD + 16],
                            t0_t[h * 64:h * 64 + KD, u, :],
                            start=True, stop=True, tile_position=(h * 64, h * 64))
                return last

            k.op("tensor", mm_p, reads=[G_b, t0_b], writes=[psb[bank]])
            for h in range(2):
                k.op("vector", lambda v, u0=u0, un=un, bank=bank, h=h:
                     v.tensor_copy(stg_t[h * 64:h * 64 + 16, u0:u0 + un, :],
                                   ps[bank][h * 64:h * 64 + 16].rearrange(
                                       "p (u c) -> p u c", c=16)),
                     reads=[psb[bank]], writes=[stg_b])

        extract(lambda h, KD=KD: stg_t[h * 64:h * 64 + 16, :, :], [stg_b])
        extract2(lambda h, KD=KD, KE=KE: G_t[h * 64 + KD:h * 64 + KE, :, KD:KD + 16], [G_b])
        k.op("vector", lambda v: v.tensor_sub(flat_t[:], flat2_t[:], flat_t[:]),
             reads=[flat2_b], writes=[flat_b])
        gj16()
        insert([lambda h: is16_t[h * 64:h * 64 + 16, :, :]], [is16_b])

        for bank, u0 in ((4, 0), (5, NP // 2)):
            un = NP // 2

            def mm_tt(t, u0=u0, un=un, bank=bank, KD=KD):
                last = None
                for h in range(2):
                    for ui in range(un):
                        u = u0 + ui
                        last = t.matmul(
                            ps[bank][h * 64:h * 64 + 16, ts(ui % 8, 64)][:, 0:KD],
                            G_t[h * 64:h * 64 + KD, u, KD:KD + 16],
                            inv_t[h * 64:h * 64 + KD, u, 0:KD],
                            start=True, stop=True, tile_position=(h * 64, h * 64))
                        if ui % 8 == 7 or ui == un - 1:
                            pass
                return last

            for w0 in range(0, un, 8):
                wn = min(8, un - w0)

                def mm_ttw(t, u0=u0, w0=w0, wn=wn, bank=bank, KD=KD):
                    last = None
                    for h in range(2):
                        for ui in range(wn):
                            u = u0 + w0 + ui
                            last = t.matmul(
                                ps[bank][h * 64:h * 64 + 16, ts(ui, 64)][:, 0:KD],
                                G_t[h * 64:h * 64 + KD, u, KD:KD + 16],
                                inv_t[h * 64:h * 64 + KD, u, 0:KD],
                                start=True, stop=True, tile_position=(h * 64, h * 64))
                    return last

                k.op("tensor", mm_ttw, reads=[G_b, inv_b], writes=[psb[bank]])
                for h in range(2):
                    k.op("vector", lambda v, u0=u0, w0=w0, wn=wn, bank=bank, KD=KD, h=h:
                         v.tensor_scalar(tt_t[h * 64:h * 64 + 16, u0 + w0:u0 + w0 + wn, 0:KD],
                                         ps[bank][h * 64:h * 64 + 16].rearrange(
                                             "p (u c) -> p u c", c=64)[:, 0:wn, 0:KD],
                                         -1.0, None, op0=ALU.mult),
                         reads=[psb[bank]], writes=[tt_b])

        k.op("vector", lambda v, KD=KD, KE=KE:
             v.tensor_copy(tt_t[:, :, KD:KE],
                           identcol_t[:, None, :].broadcast_to([P_, NP, 16])),
             reads=[identcol_b], writes=[tt_b])

        for w0 in range(0, NP, 8):
            wn = min(8, NP - w0)
            bq = 6
            bu = 7

            def mm_q2(t, w0=w0, wn=wn, bq=bq, KD=KD, KE=KE):
                last = None
                for h in range(2):
                    for ui in range(wn):
                        u = w0 + ui
                        last = t.matmul(
                            ps[bq][h * 64:h * 64 + 16, ts(ui, 64)][:, 0:KE],
                            is16_t[h * 64:h * 64 + 16, u, :],
                            tt_t[h * 64:h * 64 + 16, u, 0:KE],
                            start=True, stop=True, tile_position=(h * 64, h * 64))
                return last

            k.op("tensor", mm_q2, reads=[is16_b, tt_b], writes=[psb[bq]])
            for h in range(2):
                k.op("vector", lambda v, w0=w0, wn=wn, bq=bq, KE=KE, h=h:
                     v.tensor_copy(q2_t[h * 64:h * 64 + 16, w0:w0 + wn, 0:KE],
                                   ps[bq][h * 64:h * 64 + 16].rearrange(
                                       "p (u c) -> p u c", c=64)[:, 0:wn, 0:KE]),
                     reads=[psb[bq]], writes=[q2_b])

            def mm_upd(t, w0=w0, wn=wn, bu=bu, KE=KE):
                last = None
                for h in range(2):
                    for ui in range(wn):
                        u = w0 + ui
                        last = t.matmul(
                            ps[bu][h * 64:h * 64 + KE, ts(ui, 64)][:, 0:KE],
                            q2_t[h * 64:h * 64 + 16, u, 0:KE],
                            tt_t[h * 64:h * 64 + 16, u, 0:KE],
                            start=True, stop=True, tile_position=(h * 64, h * 64))
                return last

            k.op("tensor", mm_upd, reads=[q2_b, tt_b], writes=[psb[bu]])
            for h in range(2):
                k.op("vector", lambda v, w0=w0, wn=wn, bu=bu, h=h, KE=KE:
                     v.tensor_add(inv_t[h * 64:h * 64 + KE, w0:w0 + wn, 0:KE],
                                  inv_t[h * 64:h * 64 + KE, w0:w0 + wn, 0:KE],
                                  ps[bu][h * 64:h * 64 + KE].rearrange(
                                      "p (u c) -> p u c", c=64)[:, 0:wn, 0:KE]),
                     reads=[psb[bu]], writes=[inv_b])


def ts(i, n):
    return slice(i * n, (i + 1) * n)


def breg(g, RB, v):
    if v not in RB:
        RB[v] = g.to_reg(v)
    return RB[v]


def build(phase=2, debug=True, ffn_stop=99, probe=False):
    nc = bass.Bass("TRN2", target_bir_lowering=False)
    k = K(nc)
    st = ExitStack()

    def din(name, shape, dt=F32):
        return nc.dram_tensor(name, shape, dt, kind="ExternalInput").ap()

    def dout(name, shape, dt=F32):
        return nc.dram_tensor(name, shape, dt, kind="ExternalOutput").ap()

    def sb(stk, name, shape, dt, **kw):
        t = stk.enter_context(nc.sbuf_tensor(name, shape, dt, **kw))
        return t, Buf(name)

    x_in = din("x", [T, D])
    wq_in = din("wq", [D, D])
    wk_in = din("wk", [D, D])
    wv_in = din("wv", [D, D])
    lng_in = din("lng", [128, D])
    lnb_in = din("lnb", [128, D])
    epse_in = din("epse", [128, 512])
    wo_in = din("wo", [D, D])
    coef_in = din("coef", [128, 8])
    gjm_in = din("gjm", [128, 16, 16])
    gjm8_in = din("gjm8", [128, 8, 8])
    idc_in = din("idc", [128, 16])

    qf_dram = nc.dram_tensor("qf_scr", [8, 128, T], BF16, kind="Internal").ap()
    qfd_b = Buf("qf_dram")
    wr_dram = nc.dram_tensor("wr_scr", [4, 128, 64, 64], BF16, kind="Internal").ap()
    wrd_b = Buf("wr_dram")
    wbf_dram = nc.dram_tensor("wbf_scr", [3, 128, 8, D], BF16, kind="Internal").ap()
    wbfd_b = Buf("wbf_dram")
    bounce = nc.dram_tensor("bounce", [128, 256], F32, kind="Internal").ap()
    bounce_b = Buf("bounce")

    TT = 512
    NT = T // TT
    GP = TT // 128

    final = []

    lng_t, lng_b = sb(st, "lng_t", [128, D], F32)
    lnb_t, lnb_b = sb(st, "lnb_t", [128, D], F32)
    stats_all, statsall_b = sb(st, "stats_all", [128, 32, 2], F32)

    if debug:
        dbg_xn2 = dout("dbg_xn2", [T, D], BF16)
        dbg_ginv = dout("dbg_ginv", [128, 64, 64], F32)
        dbg_wr = dout("dbg_wr", [4, 128, 64, 64], BF16)
        if phase >= 3:
            dbg_o = dout("dbg_o", [8, 128, T], BF16)
            dbg_x2 = dout("dbg_x2", [T, D], F32)
            dbg_h2 = dout("dbg_h2", [T, D], BF16)

    k.dma("sync", lng_t[:], lng_in, writes=[lng_b], stream="w0")
    k.dma("sync", lnb_t[:], lnb_in, writes=[lnb_b], stream="w0")

    with nc.sbuf_tensor("wcast", [128, 8, D], BF16) as wcast_t:
        wcast_b = Buf("wcast")
        for i, w_in in enumerate((wq_in, wk_in, wv_in)):
            k.dma("gpsimd", wcast_t[:], w_in.rearrange("(c p) d -> p c d", p=128),
                  writes=[wcast_b], stream="gw")
            k.dma("gpsimd", wbf_dram[i], wcast_t[:], reads=[wcast_b],
                  writes=[wbfd_b], stream="gw")

    p_dram = nc.dram_tensor("p_dram", [8 * T, D], BF16, kind="Internal").ap()
    pd_b = Buf("p_dram")

    p1 = ExitStack()
    wA_t, wA_b = sb(p1, "wA", [128, 8, D], BF16)
    wB_t, wB_b = sb(p1, "wB", [128, 8, D], BF16)
    x_t, x_b = sb(p1, "x_t", [128, D], F32)
    sc1_t, sc1_b = sb(p1, "sc1", [128, D], F32)
    m1_t, m1_b = sb(p1, "m1", [128, D], F32)
    stat_t, stat_b = sb(p1, "stat", [128, GP, 8], F32)
    xn2_t, xn2_b = sb(p1, "xn2", [128, D], BF16)
    xn2f, xn2f_b = sb(p1, "xn2f", [128, 8, TT], BF16, is_transpose_dest=True)
    qtile, qtile_b = sb(p1, "qtile", [128, 8, TT], BF16)
    kv_t, kv_b = sb(p1, "kv", [128, 2, D], F32)
    G_t, G_b = sb(p1, "G_t", [128, 64, 64], F32)
    B_t, B_b = sb(p1, "B_t", [128, 64, 64], BF16)
    inv_t, inv_b = sb(p1, "inv_t", [128, 64, 64], F32)
    epse_t, epse_b = sb(p1, "epse_t", [128, 512], F32)
    gjm_t, gjm_b = sb(p1, "gjm_t", [128, 16, 16], F32)
    gjm8_t, gjm8_b = sb(p1, "gjm8_t", [128, 8, 8], F32)
    idc_t, idc_b = sb(p1, "idc_t", [128, 16], F32)
    scrgj_t, scrgj_b = sb(p1, "scrgj", [128, 300], F32)

    ps, psb = [], []
    for i in range(8):
        ps.append(st.enter_context(nc.psum_tensor(f"ps{i}", [128, 512], F32)))
        psb.append(Buf(f"ps{i}"))

    zzp_t, zzp_b = sb(p1, "zzp", [128, 2, D], BF16)
    k.op("vector", lambda v: v.memzero(zzp_t[:]), writes=[zzp_b])

    def emit_pz(j):
        for a in range(j * 16, j * 16 + 16):
            k.dma("gpsimd",
                  p_dram.rearrange("(a p) d -> a p d", p=128)[2 * a:2 * a + 2]
                  .rearrange("a p d -> p a d"),
                  zzp_t[:], reads=[zzp_b], writes=[pd_b], stream="pz")

    k.dma("sync", epse_t[:], epse_in, writes=[epse_b], stream="w0")
    k.dma("sync", gjm_t[:], gjm_in, writes=[gjm_b], stream="w0")
    k.dma("sync", gjm8_t[:], gjm8_in, writes=[gjm8_b], stream="w0")
    k.dma("sync", idc_t[:], idc_in, writes=[idc_b], stream="w0")

    def emit_batch(bb):
        k.mark(f"inv{bb}")
        k.op("vector", lambda v: v.memzero(inv_t[:]), writes=[inv_b])
        with nc.sbuf_tensor(f"t0_{bb}", [128, 64, 16], F32) as t0_t, \
             nc.sbuf_tensor(f"tt_{bb}", [128, 64, 64], F32) as tt_t, \
             nc.sbuf_tensor(f"stg_{bb}", [128, 64, 16], F32) as stg_t, \
             nc.sbuf_tensor(f"is16_{bb}", [128, 64, 16], F32) as is16_t, \
             nc.sbuf_tensor(f"q2_{bb}", [128, 64, 64], F32) as q2_t, \
             nc.sbuf_tensor(f"flat_{bb}", [128, 256], F32) as flat_t, \
             nc.sbuf_tensor(f"flat2_{bb}", [128, 256], F32) as flat2_t:
            t0_b, tt_b, stg_b = Buf("t0"), Buf("tt"), Buf("stg")
            is16_b, q2_b, flat_b, flat2_b = Buf("is16"), Buf("q2"), Buf("flat"), Buf("flat2")
            emit_border_inv(k, G_t, G_b, inv_t, inv_b, t0_t, t0_b, tt_t, tt_b,
                                 stg_t, stg_b, is16_t, is16_b, q2_t, q2_b,
                                 flat_t, flat_b, flat2_t, flat2_b, scrgj_t, scrgj_b,
                                 gjm_t, gjm_b, idc_t, idc_b, bounce, bounce_b,
                                 ps, psb, npair=64)
        k.mark(f"wr{bb}")
        if debug and bb == 0:
            final.append(k.dma("gpsimd", dbg_ginv, inv_t[:], reads=[inv_b], stream="dbg"))

        with nc.sbuf_tensor(f"invb_{bb}", [128, 64, 64], BF16) as invb_t, \
             nc.sbuf_tensor(f"zb_{bb}", [128, 64, 64], BF16) as zb_t, \
             nc.sbuf_tensor(f"g8b_{bb}", [128, 64, 8], BF16) as g8b_t, \
             nc.sbuf_tensor(f"w_{bb}", [128, 64, 64], BF16) as w_t, \
             nc.sbuf_tensor(f"wt_{bb}", [128, 64, 64], BF16) as wt_t, \
             nc.sbuf_tensor(f"m2_{bb}", [128, 64, 8], BF16) as m2_t, \
             nc.sbuf_tensor(f"m2t_{bb}", [128, 64, 64], BF16) as m2t_t, \
             nc.sbuf_tensor(f"g8_{bb}", [128, 64, 8], F32) as g8_t, \
             nc.sbuf_tensor(f"y_{bb}", [128, 64, 64], BF16) as y_t, \
             nc.sbuf_tensor(f"wr_{bb}", [128, 64, 64], BF16) as wrb_t, \
             nc.sbuf_tensor(f"flg8_{bb}", [128, 64], F32) as flg8_t:
            w_b, wt_b, m2_b, m2t_b = Buf("w"), Buf("wt"), Buf("m2"), Buf("m2t")
            g8_b, z_b, y_b, wrb_b, flg8_b = Buf("g8"), Buf("z"), Buf("y"), Buf("wrb"), Buf("flg8")
            invb_b, zb_b, g8b_b = Buf("invb"), Buf("zb"), Buf("g8b")
            k.op("vector", lambda v: v.tensor_copy(invb_t[:], inv_t[:]),
                 reads=[inv_b], writes=[invb_b])

            def waves(mm_fn, evac_fn, rbufs, banks=(0, 1)):
                for w0 in range(0, 64, 8):
                    bank = banks[(w0 // 8) % 2]
                    k.op("tensor", lambda t, w0=w0, bank=bank: mm_fn(t, w0, bank),
                         reads=list(rbufs), writes=[psb[bank]])
                    evac_fn(w0, bank)

            def full_mm(lhs_fn, rhs_fn, M):
                def mm(t, w0, bank):
                    last = None
                    for h in range(2):
                        for ui in range(8):
                            u = w0 + ui
                            last = t.matmul(
                                ps[bank][h * 64:h * 64 + M, ts(ui, 64)],
                                lhs_fn(h, u), rhs_fn(h, u),
                                start=True, stop=True, tile_position=(h * 64, h * 64))
                    return last
                return mm

            def ev_full(dst_t, dst_b):
                def ev(w0, bank):
                    k.op("vector", lambda v, w0=w0, bank=bank:
                         v.tensor_copy(dst_t[:, w0:w0 + 8, :].rearrange("p u c -> p (u c)"),
                                       ps[bank][:]),
                         reads=[psb[bank]], writes=[dst_b])
                return ev

            def ev_rows8(dst_t, dst_b):
                def ev(w0, bank):
                    for h in range(2):
                        k.op("vector", lambda v, w0=w0, bank=bank, h=h:
                             v.tensor_copy(dst_t[h * 64:h * 64 + 8, w0:w0 + 8, :],
                                           ps[bank][h * 64:h * 64 + 8].rearrange(
                                               "p (u c) -> p u c", c=64)),
                             reads=[psb[bank]], writes=[dst_b])
                return ev

            waves(full_mm(lambda h, u: invb_t[h * 64:(h + 1) * 64, u, :],
                          lambda h, u: B_t[h * 64:(h + 1) * 64, u, :], 64),
                  ev_full(w_t, w_b), [invb_b, B_b])
            waves(full_mm(lambda h, u: B_t[h * 64:(h + 1) * 64, u, :],
                          lambda h, u: invb_t[h * 64:(h + 1) * 64, u, :], 64),
                  ev_full(wt_t, wt_b), [invb_b, B_b], banks=(2, 3))

            def mmM2(t):
                last = None
                for h in range(2):
                    for u in range(64):
                        last = t.matmul(ps[4][h * 64:(h + 1) * 64, ts(u, 8)],
                                        wt_t[h * 64:(h + 1) * 64, u, :],
                                        w_t[h * 64:(h + 1) * 64, u, 0:8],
                                        start=True, stop=True,
                                        tile_position=(h * 64, h * 64))
                return last

            k.op("tensor", mmM2, reads=[wt_b, w_b], writes=[psb[4]])
            k.op("vector", lambda v:
                 v.tensor_copy(m2_t[:].rearrange("p u c -> p (u c)"), ps[4][:]),
                 reads=[psb[4]], writes=[m2_b])

            waves(full_mm(lambda h, u: w_t[h * 64:(h + 1) * 64, u, 0:8],
                          lambda h, u: wt_t[h * 64:(h + 1) * 64, u, :], 8),
                  ev_rows8(m2t_t, m2t_b), [w_b, wt_b], banks=(5, 6))

            def mmG8(t):
                last = None
                for h in range(2):
                    for u in range(64):
                        last = t.matmul(ps[7][h * 64:h * 64 + 8, ts(u, 8)],
                                        m2_t[h * 64:(h + 1) * 64, u, :],
                                        m2_t[h * 64:(h + 1) * 64, u, :],
                                        start=True, stop=True,
                                        tile_position=(h * 64, h * 64))
                return last

            k.op("tensor", mmG8, reads=[m2_b], writes=[psb[7]])
            for h in range(2):
                k.op("vector", lambda v, h=h:
                     v.tensor_copy(g8_t[h * 64:h * 64 + 8, :, :],
                                   ps[7][h * 64:h * 64 + 8].rearrange("p (u c) -> p u c", c=8)),
                     reads=[psb[7]], writes=[g8_b])

            for h in range(2):
                k.dma("sync", bounce.rearrange("(u h) (r c) -> h r u c", h=2, r=16)[h, 0:8, :, 0:8],
                      g8_t[h * 64:h * 64 + 8, :, :],
                      reads=[g8_b], writes=[bounce_b], stream="bnc")
            k.dma("sync", flg8_t[:].rearrange("p (r c) -> p r c", r=8),
                  bounce.rearrange("p (r c) -> p r c", r=16)[:, 0:8, 0:8],
                  reads=[bounce_b], writes=[flg8_b], stream="bnc")
            emit_gj(k, flg8_t[:].rearrange("p (w f) -> p w f", w=1), flg8_b, 8, 1,
                         scrgj_t, scrgj_b, gjm8_t, gjm8_b)
            k.dma("sync", bounce.rearrange("p (r c) -> p r c", r=16)[:, 0:8, 0:8],
                  flg8_t[:].rearrange("p (r c) -> p r c", r=8),
                  reads=[flg8_b], writes=[bounce_b], stream="bnc")
            for h in range(2):
                k.dma("sync", g8_t[h * 64:h * 64 + 8, :, :],
                      bounce.rearrange("(u h) (r c) -> h r u c", h=2, r=16)[h, 0:8, :, 0:8],
                      reads=[bounce_b], writes=[g8_b], stream="bnc")

            waves(full_mm(lambda h, u: m2_t[h * 64:(h + 1) * 64, u, :],
                          lambda h, u: w_t[h * 64:(h + 1) * 64, u, :], 8),
                  ev_rows8(zb_t, zb_b), [m2_b, w_b], banks=(0, 1))

            k.op("vector", lambda v: v.tensor_copy(g8b_t[:], g8_t[:]),
                 reads=[g8_b], writes=[g8b_b])
            waves(full_mm(lambda h, u: g8b_t[h * 64:h * 64 + 8, u, :],
                          lambda h, u: zb_t[h * 64:h * 64 + 8, u, :], 8),
                  ev_rows8(y_t, y_b), [g8b_b, zb_b], banks=(2, 3))

            waves(full_mm(lambda h, u: m2t_t[h * 64:h * 64 + 8, u, :],
                          lambda h, u: y_t[h * 64:h * 64 + 8, u, :], 64),
                  ev_full(wrb_t, wrb_b), [m2t_b, y_b], banks=(4, 5))

            k.dma("sync", wr_dram[bb], wrb_t[:], reads=[wrb_b], writes=[wrd_b], stream="wr")
            if debug:
                final.append(k.dma("gpsimd", dbg_wr[bb], wrb_t[:],
                                   reads=[wrb_b], stream="dbg"))

    for t_i in range(NT):
        k.mark(f"tl{t_i}")
        emit_pz(t_i)
        k.dma("sync", wA_t[:], wbf_dram[0], reads=[wbfd_b], writes=[wA_b], stream="wld")
        for gi in range(GP):
            gg = t_i * GP + gi
            xsrc = x_in.rearrange("(s p) d -> s p d", p=128)[gg]
            k.dma("sync", x_t[:], xsrc, writes=[x_b], stream="x0")
            k.op("vector",
                 lambda v, gi=gi: v.reduce_sum(stat_t[:, gi, 0:1], x_t[:], axis=AX.X),
                 reads=[x_b], writes=[stat_b])
            k.op("scalar",
                 lambda s, gi=gi: s.activation(sc1_t[:], x_t[:], AF.Square,
                                               accum_out=stat_t[:, gi, 1:2]),
                 reads=[x_b], writes=[sc1_b, stat_b])
            CC = lambda j, gi=gi: stat_t[:, gi, j:j + 1]
            k.op("vector", lambda v, CC=CC: v.tensor_scalar_mul(CC(2), CC(0), 1.0 / D),
                 reads=[stat_b], writes=[stat_b])
            k.op("vector", lambda v, CC=CC: v.tensor_scalar_mul(CC(3), CC(1), 1.0 / D),
                 reads=[stat_b], writes=[stat_b])
            k.op("vector", lambda v, CC=CC: v.tensor_mul(CC(1), CC(2), CC(2)),
                 reads=[stat_b], writes=[stat_b])
            k.op("vector", lambda v, CC=CC: v.tensor_sub(CC(3), CC(3), CC(1)),
                 reads=[stat_b], writes=[stat_b])
            k.op("vector", lambda v, CC=CC: v.tensor_scalar_add(CC(4), CC(3), EPS_LN),
                 reads=[stat_b], writes=[stat_b])
            k.op("scalar", lambda s, CC=CC: s.sqrt(CC(0), CC(4)),
                 reads=[stat_b], writes=[stat_b])
            k.op("vector", lambda v, CC=CC: v.reciprocal(CC(5), CC(0)),
                 reads=[stat_b], writes=[stat_b])
            k.op("vector", lambda v, CC=CC: v.tensor_mul(CC(1), CC(5), CC(5)),
                 reads=[stat_b], writes=[stat_b])
            k.op("vector", lambda v, CC=CC: v.tensor_mul(CC(1), CC(1), CC(3)),
                 reads=[stat_b], writes=[stat_b])
            k.op("vector", lambda v, CC=CC: v.tensor_scalar_add(CC(1), CC(1), EPS_LN),
                 reads=[stat_b], writes=[stat_b])
            k.op("scalar", lambda s, CC=CC: s.sqrt(CC(0), CC(1)),
                 reads=[stat_b], writes=[stat_b])
            k.op("vector", lambda v, CC=CC: v.reciprocal(CC(0), CC(0)),
                 reads=[stat_b], writes=[stat_b])
            k.op("vector", lambda v, CC=CC: v.tensor_mul(CC(6), CC(5), CC(0)),
                 reads=[stat_b], writes=[stat_b])
            k.op("vector", lambda v, CC=CC: v.tensor_mul(CC(7), CC(2), CC(6)),
                 reads=[stat_b], writes=[stat_b])
            k.op("vector", lambda v, CC=CC: v.tensor_scalar_mul(CC(7), CC(7), -1.0),
                 reads=[stat_b], writes=[stat_b])
            k.op("vector", lambda v, gg=gg, CC=CC: v.tensor_copy(stats_all[:, gg, 0:1], CC(2)),
                 reads=[stat_b], writes=[statsall_b])
            k.op("vector", lambda v, gg=gg, CC=CC: v.tensor_copy(stats_all[:, gg, 1:2], CC(5)),
                 reads=[stat_b], writes=[statsall_b])
            k.op("gpsimd",
                 lambda g, gi=gi: g.tensor_scalar(sc1_t[:], x_t[:],
                                                  stat_t[:, gi, 6:7], stat_t[:, gi, 7:8],
                                                  op0=ALU.mult, op1=ALU.add),
                 reads=[x_b, stat_b], writes=[sc1_b])
            k.op("gpsimd", lambda g: g.tensor_mul(m1_t[:], sc1_t[:], lng_t[:]),
                 reads=[sc1_b, lng_b], writes=[m1_b])
            k.op("gpsimd", lambda g: g.tensor_add(xn2_t[:], m1_t[:], lnb_t[:]),
                 reads=[m1_b, lnb_b], writes=[xn2_b])
            k.dma("sync", xn2f[:, :, ts(gi, 128)], xn2_t[:],
                  reads=[xn2_b], writes=[xn2f_b], stream="tr", transpose=True)
            if debug:
                final.append(k.dma("gpsimd",
                      dbg_xn2.rearrange("(s p) d -> s p d", p=128)[gg],
                      xn2_t[:], reads=[xn2_b], stream="dbg"))

        for m in range(8):
            bank = m % 4

            def mm_q(t, m=m, bank=bank):
                last = None
                for kk in range(8):
                    last = t.matmul(ps[bank][:, :], wA_t[:, kk, ts(m, 128)],
                                    xn2f[:, kk, :], start=(kk == 0), stop=(kk == 7))
                return last

            k.op("tensor", mm_q, reads=[wA_b, xn2f_b], writes=[psb[bank]])
            k.op("scalar", lambda s, m=m, bank=bank: s.copy(qtile[:, m, :], ps[bank][:]),
                 reads=[psb[bank]], writes=[qtile_b])
        k.dma("sync", qf_dram[:, :, ts(t_i, TT)].rearrange("c p n -> p c n"), qtile[:],
              reads=[qtile_b], writes=[qfd_b], stream="q0")

        k.dma("sync", wB_t[:], wbf_dram[1], reads=[wbfd_b], writes=[wB_b], stream="wld")
        k.dma("sync", wA_t[:], wbf_dram[2], reads=[wbfd_b], writes=[wA_b], stream="wld")

        for gi in range(GP):
            for half in range(2):
                bank = 4 + half

                def mm_k(t, gi=gi, half=half, bank=bank):
                    last = None
                    for kk in range(8):
                        last = t.matmul(ps[bank][:, :], xn2f[:, kk, ts(gi, 128)],
                                        wB_t[:, kk, ts(half, 512)],
                                        start=(kk == 0), stop=(kk == 7))
                    return last

                k.op("tensor", mm_k, reads=[wB_b, xn2f_b], writes=[psb[bank]])
                k.op("vector",
                     lambda v, half=half, bank=bank:
                     v.tensor_copy(kv_t[:, 0, ts(half, 512)], ps[bank][:]),
                     reads=[psb[bank]], writes=[kv_b])

                def mm_v(t, gi=gi, half=half, bank=bank):
                    last = None
                    for kk in range(8):
                        last = t.matmul(ps[bank + 2][:, :], xn2f[:, kk, ts(gi, 128)],
                                        wA_t[:, kk, ts(half, 512)],
                                        start=(kk == 0), stop=(kk == 7))
                    return last

                k.op("tensor", mm_v, reads=[wA_b, xn2f_b], writes=[psb[bank + 2]])
                k.op("vector",
                     lambda v, half=half, bank=bank:
                     v.tensor_copy(kv_t[:, 1, ts(half, 512)], ps[bank + 2][:]),
                     reads=[psb[bank + 2]], writes=[kv_b])

            if phase >= 2:
                pos = (t_i % 2) * GP + gi
                bkG, bkB = pos % 2, 2 + pos % 2

                def mm_g(t, bkG=bkG):
                    last = None
                    for lane in range(16):
                        h, lh = lane % 2, lane // 2
                        last = t.matmul(
                            ps[bkG][h * 64:(h + 1) * 64, ts(lh, 64)],
                            kv_t[:, 0, ts(lane, 64)], kv_t[:, 0, ts(lane, 64)],
                            start=True, stop=True, tile_position=(0, h * 64))
                    return last

                k.op("tensor", mm_g, reads=[kv_b], writes=[psb[bkG]])
                k.op("vector", lambda v, pos=pos, bkG=bkG:
                     v.tensor_add(G_t[:, ts(pos, 8), :].rearrange("p u c -> p (u c)"),
                                  ps[bkG][:], epse_t[:]),
                     reads=[psb[bkG], epse_b], writes=[G_b])

                def mm_b(t, bkB=bkB):
                    last = None
                    for lane in range(16):
                        h, lh = lane % 2, lane // 2
                        last = t.matmul(
                            ps[bkB][h * 64:(h + 1) * 64, ts(lh, 64)],
                            kv_t[:, 0, ts(lane, 64)], kv_t[:, 1, ts(lane, 64)],
                            start=True, stop=True, tile_position=(0, h * 64))
                    return last

                k.op("tensor", mm_b, reads=[kv_b], writes=[psb[bkB]])
                k.op("vector", lambda v, pos=pos, bkB=bkB:
                     v.tensor_copy(B_t[:, ts(pos, 8), :].rearrange("p u c -> p (u c)"),
                                   ps[bkB][:]),
                     reads=[psb[bkB]], writes=[B_b])

        if phase >= 2 and t_i % 2 == 1:
            emit_batch(t_i // 2)

    p1.close()
    k.fence()

    if phase >= 3:
        k.mark("phase3")
        L = dict(locals())
        emit_phase3(nc, k, st, final, debug, L)
        if phase >= 4:
            emit_ffn(nc, k, st, final, debug, L, stop=L.get("ffn_stop", 99) if isinstance(L, dict) else 99)

    k.finalize(final)
    st.close()
    return nc





def emit_phase3(nc, k, st, final, debug, L):
    sb = L["sb"]
    ps, psb = L["ps"], L["psb"]
    wr_dram, wrd_b = L["wr_dram"], L["wrd_b"]
    qf_dram, qfd_b = L["qf_dram"], L["qfd_b"]
    x_in = L["x_in"]
    wo_in = L["wo_in"]
    coef_in = L["coef_in"]
    stats_all, statsall_b = L["stats_all"], L["statsall_b"]
    dout = L["dout"]
    din = L["din"]
    if debug:
        dbg_o, dbg_x2, dbg_h2 = L["dbg_o"], L["dbg_x2"], L["dbg_h2"]

    p2 = ExitStack()
    wo_t, wo_b = sb(p2, "wo_t", [128, 8, D], BF16)
    coef_t, coef_b = sb(p2, "coef_t", [128, 8], F32)
    sstate, sstate_b = sb(p2, "sstate", [128, 8, 64], F32)
    s_all, sall_b = sb(p2, "s_all", [128, 2, 512], BF16)
    s_dram = nc.dram_tensor("s_scr", [32, 128, 512], BF16, kind="Internal").ap()
    sd_b = Buf("s_dram")
    carry, carry_b = sb(p2, "carry", [128, 512], F32)
    fj, fj_b = sb(p2, "fj", [128, 512], F32)
    scorr, scorr_b = sb(p2, "scorr", [128, 8, 64], BF16)
    wrw, wrw_b = sb(p2, "wrw", [128, 2, 8, 64], BF16)
    qt_t, qt_b = sb(p2, "qt", [128, 8, 128], BF16)
    o_t, o_b = sb(p2, "o_t", [128, 8, T], BF16)

    ag1_in = nc.dram_tensor("ag1_in", [128, 512], F32, kind="Internal").ap()
    ag1_out = nc.dram_tensor("ag1_out", [8, 128, 512], F32, kind="Internal").ap()
    ag1i_b, ag1o_b = Buf("ag1_in"), Buf("ag1_out")
    x2_dram = nc.dram_tensor("x2_scr", [T, D], F32, kind="Internal").ap()
    x2d_b = Buf("x2_dram")
    h2_dram = nc.dram_tensor("h2_scr", [T, D], BF16, kind="Internal").ap()
    h2d_b = Buf("h2_dram")

    k.dma("gpsimd", wo_t[:], wo_in.rearrange("(c p) d -> p c d", p=128),
          writes=[wo_b], stream="gw")
    k.dma("sync", coef_t[:], coef_in, writes=[coef_b], stream="w0")

    k.mark("scan")
    k.op("vector", lambda v: v.memzero(sstate[:]), writes=[sstate_b])
    wrwb2 = [Buf("wrw0"), Buf("wrw1")]
    for c in range(32):
        bb, pos = c // 8, c % 8
        k.dma("sync", wrw[:, c % 2], wr_dram[bb][:, ts(pos, 8), :], reads=[wrd_b],
              writes=[wrwb2[c % 2]], stream="wrl")
        k.op("vector", lambda v, c=c: v.scalar_tensor_tensor(
                 sstate[:].rearrange("p a b -> p (a b)"),
                 sstate[:].rearrange("p a b -> p (a b)"), ALPHA,
                 wrw[:, c % 2].rearrange("p a b -> p (a b)"),
                 op0=ALU.mult, op1=ALU.add),
             reads=[wrwb2[c % 2], sstate_b], writes=[sstate_b])
        k.op("vector", lambda v, c=c: v.tensor_copy(
                 s_all[:, c % 2, :], sstate[:].rearrange("p a b -> p (a b)")),
             reads=[sstate_b], writes=[sall_b])
        k.dma("sync", s_dram[c], s_all[:, c % 2, :], reads=[sall_b], writes=[sd_b],
              stream="ss")

    k.dma("sync", ag1_in, sstate[:].rearrange("p a b -> p (a b)"),
          reads=[sstate_b], writes=[ag1i_b], stream="ag")
    k.op("gpsimd", lambda g: g.collective_compute(
             "AllGather", ALU.bypass, ins=[ag1_in], outs=[ag1_out],
             replica_groups=[[0, 1, 2, 3, 4, 5, 6, 7]]),
         reads=[ag1i_b], writes=[ag1o_b])

    k.op("vector", lambda v: v.memzero(carry[:]), writes=[carry_b])
    for j in range(8):
        k.dma("sync", fj[:], ag1_out[j], reads=[ag1o_b], writes=[fj_b], stream="agl")
        k.op("vector", lambda v, j=j: v.scalar_tensor_tensor(
                 carry[:], fj[:], coef_t[:, j:j + 1], carry[:],
                 op0=ALU.mult, op1=ALU.add),
             reads=[fj_b, coef_b, carry_b], writes=[carry_b])

    k.mark("out2")
    for c in range(32):
        k.dma("sync", s_all[:, c % 2, :], s_dram[c], reads=[sd_b], writes=[sall_b],
              stream="ss")
        k.op("vector", lambda v, c=c: v.scalar_tensor_tensor(
                 scorr[:].rearrange("p a b -> p (a b)"),
                 carry[:], float(ALPHA ** (c + 1)), s_all[:, c % 2, :],
                 op0=ALU.mult, op1=ALU.add),
             reads=[carry_b, sall_b], writes=[scorr_b])
        k.dma("sync", qt_t[:], qf_dram[:, :, ts(c, 128)].rearrange("c p n -> p c n"),
              reads=[qfd_b], writes=[qt_b], stream="ql")
        for bi in range(4):
            def mm_o(t, c=c, bi=bi):
                last = None
                for sl in range(2):
                    lh = bi + sl * 4
                    for h in range(2):
                        last = t.matmul(
                            ps[bi][h * 64:(h + 1) * 64, ts(sl, 128)],
                            scorr[h * 64:(h + 1) * 64, lh, :],
                            qt_t[h * 64:(h + 1) * 64, lh, :],
                            start=True, stop=True, tile_position=(h * 64, h * 64))
                return last

            k.op("tensor", mm_o, reads=[scorr_b, qt_b], writes=[psb[bi]])
            k.op("vector", lambda v, c=c, bi=bi: v.tensor_scalar(
                     o_t[:, bi:8:4, ts(c, 128)],
                     ps[bi][:].rearrange("p (sl n) -> p sl n", n=128)[:, 0:2, :],
                     SCALE, None, op0=ALU.mult),
                 reads=[psb[bi]], writes=[o_b])

    if debug:
        final.append(k.dma("gpsimd", dbg_o.rearrange("c p n -> p c n"), o_t[:],
                           reads=[o_b], stream="dbg"))

    nmr, nmr_b = sb(p2, "nmr", [128, 32], F32)
    k.op("vector", lambda v: v.tensor_mul(nmr[:], stats_all[:, :, 0], stats_all[:, :, 1]),
         reads=[statsall_b], writes=[nmr_b])
    k.op("vector", lambda v: v.tensor_scalar_mul(nmr[:], nmr[:], -1.0),
         reads=[nmr_b], writes=[nmr_b])

    k.mark("x2h2")
    xr_t, xr_b = sb(p2, "xr", [128, D], F32)
    hh_t, hh_b = sb(p2, "hh", [128, D], F32)
    x2_t, x2_b = sb(p2, "x2t", [128, D], F32)
    h2_t, h2_b = sb(p2, "h2t", [128, D], BF16)
    st2_t, st2_b = sb(p2, "st2", [128, 8], F32)
    sq2_t, sq2_b = sb(p2, "sq2", [128, D], F32)

    for tb in range(32):
        k.dma("sync", xr_t[:], x_in.rearrange("(s p) d -> s p d", p=128)[tb],
              writes=[xr_b], stream="x2l")
        for dh in range(2):
            bank = 4 + dh

            def mm_wo(t, tb=tb, dh=dh, bank=bank):
                last = None
                for fc in range(8):
                    last = t.matmul(ps[bank][:, :], o_t[:, fc, ts(tb, 128)],
                                    wo_t[:, fc, ts(dh, 512)],
                                    start=(fc == 0), stop=(fc == 7))
                return last

            k.op("tensor", mm_wo, reads=[o_b, wo_b], writes=[psb[bank]])
        k.op("gpsimd", lambda g, tb=tb: g.tensor_scalar(
                 hh_t[:], xr_t[:], stats_all[:, tb, 1:2], nmr[:, tb:tb + 1],
                 op0=ALU.mult, op1=ALU.add),
             reads=[xr_b, statsall_b, nmr_b], writes=[hh_b])
        k.op("gpsimd", lambda g: g.tensor_add(hh_t[:], hh_t[:], xr_t[:]),
             reads=[xr_b, hh_b], writes=[hh_b])
        for dh in range(2):
            bank = 4 + dh
            k.op("vector", lambda v, dh=dh, bank=bank: v.tensor_add(
                     x2_t[:, ts(dh, 512)], hh_t[:, ts(dh, 512)], ps[bank][:]),
                 reads=[hh_b, psb[bank]], writes=[x2_b])
        k.dma("sync", x2_dram.rearrange("(s p) d -> s p d", p=128)[tb], x2_t[:],
              reads=[x2_b], writes=[x2d_b], stream="x2s")
        k.op("vector", lambda v: v.reduce_sum(st2_t[:, 0:1], x2_t[:], axis=AX.X),
             reads=[x2_b], writes=[st2_b])
        k.op("scalar", lambda s: s.activation(sq2_t[:], x2_t[:], AF.Square,
                                              accum_out=st2_t[:, 1:2]),
             reads=[x2_b], writes=[sq2_b, st2_b])
        C2 = lambda j: st2_t[:, j:j + 1]
        k.op("vector", lambda v: v.tensor_scalar_mul(C2(2), C2(0), 1.0 / D),
             reads=[st2_b], writes=[st2_b])
        k.op("vector", lambda v: v.tensor_scalar_mul(C2(3), C2(1), 1.0 / D),
             reads=[st2_b], writes=[st2_b])
        k.op("vector", lambda v: v.tensor_mul(C2(1), C2(2), C2(2)),
             reads=[st2_b], writes=[st2_b])
        k.op("vector", lambda v: v.tensor_sub(C2(3), C2(3), C2(1)),
             reads=[st2_b], writes=[st2_b])
        k.op("vector", lambda v: v.tensor_scalar_add(C2(4), C2(3), EPS_LN),
             reads=[st2_b], writes=[st2_b])
        k.op("scalar", lambda s: s.sqrt(C2(5), C2(4)), reads=[st2_b], writes=[st2_b])
        k.op("vector", lambda v: v.reciprocal(C2(5), C2(5)), reads=[st2_b], writes=[st2_b])
        k.op("vector", lambda v: v.tensor_mul(C2(6), C2(2), C2(5)),
             reads=[st2_b], writes=[st2_b])
        k.op("vector", lambda v: v.tensor_scalar_mul(C2(6), C2(6), -1.0),
             reads=[st2_b], writes=[st2_b])
        k.op("gpsimd", lambda g: g.tensor_scalar(
                 h2_t[:], x2_t[:], C2(5), C2(6), op0=ALU.mult, op1=ALU.add),
             reads=[x2_b, st2_b], writes=[h2_b])
        k.dma("sync", h2_dram.rearrange("(s p) d -> s p d", p=128)[tb], h2_t[:],
              reads=[h2_b], writes=[h2d_b], stream="h2s")
        if debug:
            final.append(k.dma("gpsimd",
                  dbg_x2.rearrange("(s p) d -> s p d", p=128)[tb], x2_t[:],
                  reads=[x2_b], stream="dbg"))
            final.append(k.dma("gpsimd",
                  dbg_h2.rearrange("(s p) d -> s p d", p=128)[tb], h2_t[:],
                  reads=[h2_b], stream="dbg"))

    p2.close()
    k.fence()
    L["phase3_out"] = dict(x2_dram=x2_dram, x2d_b=x2d_b, h2_dram=h2_dram, h2d_b=h2d_b)


def emit_ffn(nc, k, st, final, debug, L, stop=99):
    sb = L["sb"]
    din, dout = L["din"], L["dout"]
    ps, psb = L["ps"], L["psb"]
    p3 = L["phase3_out"]
    h2_dram, h2d_b = p3["h2_dram"], p3["h2d_b"]
    x2_dram, x2d_b = p3["x2_dram"], p3["x2d_b"]
    from concourse.bass_core_rust import IndirectOffsetOnAxis

    gwT_in = din("gwT", [D, 16])
    wsgT_in = din("wsgT", [D, 2048])
    wsuT_in = din("wsuT", [D, 2048])
    wsdT_in = din("wsdT", [2048, D])
    wgT_in = [din(f"wgT{i}", [D, 2048]) for i in range(4)]
    wuT_in = [din(f"wuT{i}", [D, 2048]) for i in range(4)]
    wdT_in = [din(f"wdT{i}", [2048, D]) for i in range(4)]
    ident_in = din("ident", [128, 128])
    kth4_in = din("kth4", [128, 4], I32)
    iota32r_in = din("iota32r", [128, 32])
    tloc_in = din("tloc", [128, 128])
    tril_in = din("tril", [128, 128], BF16)
    ones1_in = din("ones1", [1, 128])
    ones128_in = din("ones128", [128, 1])
    ones128b_in = din("ones128b", [128, 1], BF16)

    out_final = dout("out", [T, D], F32)
    outf_b = Buf("out")

    ag2s_in = nc.dram_tensor("ag2s_in", [16, T], F32, kind="Internal").ap()
    ag2s_out = nc.dram_tensor("ag2s_out", [8, 16, T], F32, kind="Internal").ap()
    ag2h_out = nc.dram_tensor("ag2h_out", [8, T, D], BF16, kind="Internal").ap()
    p_dram, pd_b = L["p_dram"], L["pd_b"]
    rs_out = nc.dram_tensor("rs_out", [T, D], BF16, kind="Internal").ap()
    auxc_dram = [nc.dram_tensor(f"auxc{i}", [2176, 4], F32, kind="Internal").ap()
                 for i in range(4)]
    sh_dram = nc.dram_tensor("sh_scr", [T, D], F32, kind="Internal").ap()
    ag2si_b, ag2so_b = Buf("ag2s_in"), Buf("ag2s_out")
    ag2ho_b = Buf("ag2h_out")
    rso_b = Buf("rs_out")
    auxd_b = Buf("auxc_dram")
    shd_b = Buf("sh_dram")

    G4 = [[0, 1, 2, 3, 4, 5, 6, 7]]
    pf = ExitStack()
    pa = ExitStack()

    ident_t, ident_b = sb(pf, "ident_t", [128, 128], F32)
    ones1_t, ones1_b = sb(pf, "ones1_t", [1, 128], F32)
    ones128b_t, ones128b_b = sb(pf, "ones128b_t", [128, 1], BF16)
    tril_t, tril_b = sb(pf, "tril_t", [128, 128], BF16)
    kth_t, kth_b = sb(pf, "kth_t", [128, 4, 128], F32)
    nxt4_t, nxt4_b = sb(pf, "nxt4", [128, 4], F32)
    iota32r_t, iota32r_b = sb(pf, "iota32r_t", [128, 32], F32)
    tloc_t, tloc_b = sb(pf, "tloc_t", [128, 128], F32)
    k.dma("sync", ident_t[:], ident_in, writes=[ident_b], stream="w0")
    k.dma("sync", ones1_t[:], ones1_in, writes=[ones1_b], stream="w0")
    k.dma("sync", ones128b_t[:], ones128b_in, writes=[ones128b_b], stream="w0")
    k.dma("sync", tril_t[:], tril_in, writes=[tril_b], stream="w0")
    k.dma("sync", iota32r_t[:], iota32r_in, writes=[iota32r_b], stream="w0")
    k.dma("sync", tloc_t[:], tloc_in, writes=[tloc_b], stream="w0")

    k.mark("gateA")
    gw_t, gw_b = sb(pa, "gw_t", [128, 8, 16], BF16)
    scomb, scomb_b = sb(pa, "scomb", [128, 32, 16], F32)
    red_t, red_b = sb(pa, "red_t", [128, 4, 32], F32)
    sct_t, sct_b = sb(pa, "sct", [16, 32, 128], F32)
    h2row, h2row_b = sb(pa, "h2row", [128, 2, D], BF16)
    h2ft, h2ft_b = sb(pa, "h2ft", [128, 2, 8, 128], BF16, is_transpose_dest=True)

    k.dma("gpsimd", gw_t[:], gwT_in.rearrange("(c p) e -> p c e", p=128),
          writes=[gw_b], stream="gw")

    for tb in range(32):
        k.dma("sync", h2row[:, tb % 2, :],
              h2_dram.rearrange("(s p) d -> s p d", p=128)[tb],
              reads=[h2d_b], writes=[h2row_b], stream="xh")
        k.dma("sync", h2ft[:, tb % 2, :, :],
              h2row[:, tb % 2, :].rearrange("p (c n) -> p c n", n=128),
              reads=[h2row_b], writes=[h2ft_b], stream="tr", transpose=True)

        def mm_gate(t, tb=tb):
            last = None
            for kk in range(8):
                last = t.matmul(ps[0][:, ts(tb, 16)], h2ft[:, tb % 2, kk, :],
                                gw_t[:, kk, :], start=(kk == 0), stop=(kk == 7))
            return last

        k.op("tensor", mm_gate, reads=[h2ft_b, gw_b], writes=[psb[0]])
    k.op("vector", lambda v: v.tensor_copy(scomb[:].rearrange("p a e -> p (a e)"),
                                           ps[0][:]),
         reads=[psb[0]], writes=[scomb_b])
    k.op("vector", lambda v: v.tensor_reduce(red_t[:, 0, :, None], scomb[:], axis=AX.X,
                                             op=ALU.max),
         reads=[scomb_b], writes=[red_b])
    k.op("vector", lambda v: v.scalar_tensor_tensor(
             scomb[:], scomb[:], 1.0, red_t[:, 0, :, None].broadcast_to([128, 32, 16]),
             op0=ALU.mult, op1=ALU.subtract),
         reads=[scomb_b, red_b], writes=[scomb_b])
    k.op("scalar", lambda s: s.activation(scomb[:].rearrange("p a e -> p (a e)"),
                                          scomb[:].rearrange("p a e -> p (a e)"), AF.Exp),
         reads=[scomb_b], writes=[scomb_b])
    k.op("vector", lambda v: v.tensor_reduce(red_t[:, 1, :, None], scomb[:], axis=AX.X,
                                             op=ALU.add),
         reads=[scomb_b], writes=[red_b])
    k.op("vector", lambda v: v.reciprocal(red_t[:, 2, :], red_t[:, 1, :]),
         reads=[red_b], writes=[red_b])
    k.op("vector", lambda v: v.tensor_mul(scomb[:], scomb[:],
                                          red_t[:, 2, :, None].broadcast_to([128, 32, 16])),
         reads=[scomb_b, red_b], writes=[scomb_b])

    for rr in range(8):
        def mm_tr(t, rr=rr):
            last = None
            for i in range(4):
                last = t.transpose(ps[1][0:16, ts(i, 128)], scomb[:, rr * 4 + i, :],
                                   ident_t[:])
            return last

        k.op("tensor", mm_tr, reads=[scomb_b, ident_b], writes=[psb[1]])
        k.op("vector", lambda v, rr=rr:
             v.tensor_copy(sct_t[0:16, ts(rr, 4), :],
                           ps[1][0:16, :].rearrange("p (a n) -> p a n", n=128)),
             reads=[psb[1]], writes=[sct_b])
    k.dma("sync", ag2s_in, sct_t[:].rearrange("p a n -> p (a n)"),
          reads=[sct_b], writes=[ag2si_b], stream="ag")
    k.op("gpsimd", lambda g: g.collective_compute(
              "AllGather", ALU.bypass, ins=[ag2s_in], outs=[ag2s_out],
              replica_groups=G4),
          reads=[ag2si_b], writes=[ag2so_b])
    k.op("gpsimd", lambda g: g.collective_compute(
              "AllGather", ALU.bypass, ins=[h2_dram], outs=[ag2h_out],
              replica_groups=G4),
          reads=[h2d_b], writes=[ag2ho_b],
         extra=[Tok(k._stream("h2s"), 0, "h2s")])
    if stop <= 1:
        pa.close()
        pf.close()
        return

    k.mark("bisect")
    kidx_t, kidx_b = sb(pa, "kidx", [128, 4], I32)
    tau_t, tau_b = sb(pa, "tau", [1, 8], F32)
    ones128_t, ones128_b = sb(pa, "ones128_t", [128, 1], F32)
    bis_t, bis_b = sb(pa, "bis_t", [1, 4, 4], F32)
    redc_t, redc_b = sb(pa, "redc_t", [128, 4], F32)
    cmps_t, cmps_b = sb(pa, "cmps_t", [128, 128], F32)
    mk_t, mk_b = sb(pa, "mk_t", [1, 4, 2], U32)
    k.dma("sync", ones128_t[:], ones128_in, writes=[ones128_b], stream="w0")
    k.dma("sync", kidx_t[:], kth4_in, writes=[kidx_b], stream="w0")
    for pl in range(4):
        k.dop("gpsimd", lambda g, pl=pl: g.indirect_dma_start(
                  kth_t[:, pl, :], None,
                  ag2s_out.rearrange("c e (s n) -> (c e s) n", n=128),
                  IndirectOffsetOnAxis(kidx_t[:, pl:pl + 1], 0)),
              reads=[kidx_b, ag2so_b], writes=[kth_b], stream="gg")

    LO = lambda: bis_t[0:1, :, 0]
    HI = lambda: bis_t[0:1, :, 1]
    MID = lambda: bis_t[0:1, :, 2]
    NXT = lambda: bis_t[0:1, :, 3]
    k.op("vector", lambda v: v.memset(bis_t[0:1, :, 0], 0.0), writes=[bis_b])
    k.op("vector", lambda v: v.memset(bis_t[0:1, :, 1], 1.0), writes=[bis_b])
    k.op("vector", lambda v: v.memset(bis_t[0:1, :, 2], 0.5), writes=[bis_b])

    def count_round(src_fn):
        k.op("tensor", lambda t: t.matmul(ps[2][:, 0:4], ones1_t[:], src_fn(),
                                          start=True, stop=True),
             reads=[bis_b, ones1_b], writes=[psb[2]])
        for pl in range(4):
            k.op("vector", lambda v, pl=pl: v.tensor_scalar(
                     cmps_t[:], kth_t[:, pl, :], ps[2][:, pl:pl + 1], None,
                     op0=ALU.is_gt, op1=ALU.add, accum_out=redc_t[:, pl:pl + 1]),
                 reads=[kth_b, psb[2]], writes=[cmps_b, redc_b])
        k.op("tensor", lambda t: t.matmul(ps[3][0:1, 0:4], ones128_t[:], redc_t[:],
                                          start=True, stop=True),
             reads=[redc_b, ones128_b], writes=[psb[3]])

    for it in range(34):
        count_round(MID)
        k.op("vector", lambda v: v.tensor_scalar(mk_t[0:1, :, 0], ps[3][0:1, 0:4],
                                                 float(CAP), None, op0=ALU.is_ge),
             reads=[psb[3]], writes=[mk_b])
        k.op("vector", lambda v: v.tensor_scalar(mk_t[0:1, :, 1], ps[3][0:1, 0:4],
                                                 float(CAP), None, op0=ALU.is_lt),
             reads=[psb[3]], writes=[mk_b])
        k.op("vector", lambda v: v.copy_predicated(LO(), mk_t[0:1, :, 0], MID()),
             reads=[mk_b, bis_b], writes=[bis_b])
        k.op("vector", lambda v: v.copy_predicated(HI(), mk_t[0:1, :, 1], MID()),
             reads=[mk_b, bis_b], writes=[bis_b])
        k.op("vector", lambda v: v.tensor_add(MID(), LO(), HI()),
             reads=[bis_b], writes=[bis_b])
        k.op("vector", lambda v: v.tensor_scalar_mul(MID(), MID(), 0.5),
             reads=[bis_b], writes=[bis_b])
    count_round(LO)
    k.op("vector", lambda v: v.tensor_scalar(mk_t[0:1, :, 0], ps[3][0:1, 0:4],
                                             float(CAP), None, op0=ALU.is_equal),
         reads=[psb[3]], writes=[mk_b])
    k.op("vector", lambda v: v.tensor_copy(NXT(), HI()), reads=[bis_b], writes=[bis_b])
    k.op("vector", lambda v: v.copy_predicated(NXT(), mk_t[0:1, :, 0], LO()),
         reads=[mk_b, bis_b], writes=[bis_b])
    k.op("vector", lambda v: v.tensor_copy(tau_t[0:1, 0:4], NXT()),
         reads=[bis_b], writes=[tau_b])
    k.op("tensor", lambda t: t.matmul(ps[2][:, 40:44], ones1_t[:], tau_t[0:1, 0:4],
                                      start=True, stop=True),
         reads=[tau_b, ones1_b], writes=[psb[2]])
    k.op("vector", lambda v: v.tensor_copy(nxt4_t[:], ps[2][:, 40:44]),
         reads=[psb[2]], writes=[nxt4_b])
    if debug:
        dbg_tau = dout("dbg_tau", [1, 8], F32)
        final.append(k.dma("gpsimd", dbg_tau, tau_t[:], reads=[tau_b], stream="dbg"))
    if stop <= 2:
        pa.close()
        pf.close()
        return

    pa.close()
    k.fence()
    EX = dict(
        ident_t=ident_t, ident_b=ident_b, ones1_t=ones1_t, ones1_b=ones1_b,
        ones128b_t=ones128b_t, ones128b_b=ones128b_b, tril_t=tril_t, tril_b=tril_b,
        kth_t=kth_t, kth_b=kth_b, nxt4_t=nxt4_t, nxt4_b=nxt4_b,
        iota32r_t=iota32r_t,
        iota32r_b=iota32r_b, tloc_t=tloc_t, tloc_b=tloc_b,
        wsgT_in=wsgT_in, wsuT_in=wsuT_in, wsdT_in=wsdT_in,
        wgT_in=wgT_in, wuT_in=wuT_in, wdT_in=wdT_in,
        ag2h_out=ag2h_out, ag2ho_b=ag2ho_b,
        p_dram=p_dram, pd_b=pd_b, rs_out=rs_out, rso_b=rso_b,
        auxc_dram=auxc_dram, auxd_b=auxd_b, sh_dram=sh_dram, shd_b=shd_b,
        out_final=out_final, outf_b=outf_b, G4=G4, IOA=IndirectOffsetOnAxis,
        RB={},
    )
    emit_experts(nc, k, st, final, debug, L, EX)
    if stop <= 5:
        pf.close()
        return
    emit_shared(nc, k, st, final, debug, L, EX)
    if stop <= 6:
        pf.close()
        return
    emit_final(nc, k, st, final, debug, L, EX)
    pf.close()


def emit_shared(nc, k, st, final, debug, L, EX):
    k.mark("shared")
    sb = L["sb"]
    ps, psb = L["ps"], L["psb"]
    p3 = L["phase3_out"]
    h2_dram, h2d_b = p3["h2_dram"], p3["h2d_b"]
    sh_dram, shd_b = EX["sh_dram"], EX["shd_b"]

    pe = ExitStack()
    wsg_t, wsg_b = sb(pe, "wsg_t", [128, 8, 2048], BF16)
    wsu_t, wsu_b = sb(pe, "wsu_t", [128, 8, 2048], BF16)
    wsd_t, wsd_b = sb(pe, "wsd_t", [128, 16, D], BF16)
    h2r2, h2r2_b = sb(pe, "h2r2", [128, 2, D], BF16)
    h2f2, h2f2_b = sb(pe, "h2f2", [128, 2, 8, 512], BF16, is_transpose_dest=True)
    gsh_t, gsh_b = sb(pe, "gsh", [128, 512], BF16)
    ush_t, ush_b = sb(pe, "ush", [128, 512], BF16)
    hsh_t, hsh_b = sb(pe, "hsh", [128, 16, 512], BF16)
    shs_t, shs_b = sb(pe, "shs", [128, 2, D], F32)

    k.dma("gpsimd", wsg_t[:], EX["wsgT_in"].rearrange("(c p) f -> p c f", p=128),
          writes=[wsg_b], stream="gw")
    k.dma("gpsimd", wsu_t[:], EX["wsuT_in"].rearrange("(c p) f -> p c f", p=128),
          writes=[wsu_b], stream="gw")
    k.dma("gpsimd", wsd_t[:], EX["wsdT_in"].rearrange("(c p) d -> p c d", p=128),
          writes=[wsd_b], stream="gw")
    k.op("gpsimd", lambda g: g.collective_compute(
             "ReduceScatter", ALU.add, ins=[EX["p_dram"]], outs=[EX["rs_out"]],
             replica_groups=EX["G4"]),
         reads=[EX["pd_b"]], writes=[EX["rso_b"]])

    def load_tile(tt):
        for rb in range(4):
            tb = tt * 4 + rb
            k.dma("sync", h2r2[:, tb % 2, :],
                  h2_dram.rearrange("(s p) d -> s p d", p=128)[tb],
                  reads=[h2d_b], writes=[h2r2_b], stream="xh")
            k.dma("sync", h2f2[:, tt % 2, :, ts(rb, 128)],
                  h2r2[:, tb % 2, :].rearrange("p (c n) -> p c n", n=128),
                  reads=[h2r2_b], writes=[h2f2_b], stream="tr", transpose=True)

    load_tile(0)
    for tt in range(8):
        for j in range(16):
            pj = j % 2

            def mm_g(t, tt=tt, j=j, pj=pj):
                last = None
                for kk in range(8):
                    last = t.matmul(ps[pj][:, :], wsg_t[:, kk, ts(j, 128)],
                                    h2f2[:, tt % 2, kk, :],
                                    start=(kk == 0), stop=(kk == 7))
                return last

            k.op("tensor", mm_g, reads=[wsg_b, h2f2_b], writes=[psb[pj]])
            k.op("scalar", lambda s, pj=pj: s.activation(gsh_t[:], ps[pj][:], AF.Gelu),
                 reads=[psb[pj]], writes=[gsh_b])

            def mm_u(t, tt=tt, j=j, pj=pj):
                last = None
                for kk in range(8):
                    last = t.matmul(ps[2 + pj][:, :], wsu_t[:, kk, ts(j, 128)],
                                    h2f2[:, tt % 2, kk, :],
                                    start=(kk == 0), stop=(kk == 7))
                return last

            k.op("tensor", mm_u, reads=[wsu_b, h2f2_b], writes=[psb[2 + pj]])
            k.op("scalar", lambda s, pj=pj: s.copy(ush_t[:], ps[2 + pj][:]),
                 reads=[psb[2 + pj]], writes=[ush_b])
            k.op("vector", lambda v, j=j: v.tensor_mul(hsh_t[:, j, :], gsh_t[:],
                                                       ush_t[:]),
                 reads=[gsh_b, ush_b], writes=[hsh_b])
        if tt + 1 < 8:
            load_tile(tt + 1)
        for rb in range(4):
            tb = tt * 4 + rb
            for half in range(2):
                bank = 4 + half

                def mm_dn(t, rb=rb, half=half, bank=bank):
                    last = None
                    for j in range(16):
                        last = t.matmul(ps[bank][:, :], hsh_t[:, j, ts(rb, 128)],
                                        wsd_t[:, j, ts(half, 512)],
                                        start=(j == 0), stop=(j == 15))
                    return last

                k.op("tensor", mm_dn, reads=[hsh_b, wsd_b], writes=[psb[bank]])
                k.op("vector", lambda v, tb=tb, half=half, bank=bank:
                     v.tensor_copy(shs_t[:, tb % 2, ts(half, 512)], ps[bank][:]),
                     reads=[psb[bank]], writes=[shs_b])
            k.dma("sync", sh_dram.rearrange("(s p) d -> s p d", p=128)[tb],
                  shs_t[:, tb % 2, :], reads=[shs_b], writes=[shd_b], stream="shs")
    if debug:
        dout = L["dout"]
        dbg_sh = dout("dbg_sh", [T, D], F32)
        final.append(k.dma("gpsimd", dbg_sh, sh_dram, reads=[shd_b], stream="dbg"))
    pe.close()
    k.fence()


def emit_experts(nc, k, st, final, debug, L, EX):
    k.mark("experts")
    sb = L["sb"]
    ps, psb = L["ps"], L["psb"]
    dout = L["dout"]
    IOA = EX["IOA"]
    kth_t, kth_b = EX["kth_t"], EX["kth_b"]
    nxt4_t, nxt4_b = EX["nxt4_t"], EX["nxt4_b"]
    ident_t, ident_b = EX["ident_t"], EX["ident_b"]
    ones1_t, ones1_b = EX["ones1_t"], EX["ones1_b"]
    ones128b_t, ones128b_b = EX["ones128b_t"], EX["ones128b_b"]
    tril_t, tril_b = EX["tril_t"], EX["tril_b"]
    iota32r_t, iota32r_b = EX["iota32r_t"], EX["iota32r_b"]
    tloc_t, tloc_b = EX["tloc_t"], EX["tloc_b"]
    ag2h_out, ag2ho_b = EX["ag2h_out"], EX["ag2ho_b"]
    p_dram, pd_b = EX["p_dram"], EX["pd_b"]
    auxc_dram, auxd_b = EX["auxc_dram"], EX["auxd_b"]
    wgT_in, wuT_in, wdT_in = EX["wgT_in"], EX["wuT_in"], EX["wdT_in"]
    G4 = EX["G4"]
    RB = EX["RB"]

    pe = ExitStack()
    sc_std, scstd_b = sb(pe, "sc_std", [128, 128], F32)
    mpf_t, mpf_b = sb(pe, "mpf", [128, 128], F32)
    mpb_t, mpb_b = sb(pe, "mpb", [128, 128], BF16)
    p1sb_t, p1sb_b = sb(pe, "p1sb", [128, 128], F32)
    csr2_t, csr2_b = sb(pe, "csr2", [1, 256], F32)
    grank_t, grank_b = sb(pe, "grank", [128, 128], F32)
    scat_t, scat_b = sb(pe, "scat", [128, 128], F32)
    lsel_t, lsel_b = sb(pe, "lsel", [128, 128], F32)
    self_f, selff_b = sb(pe, "self_f", [128, 4, 32], F32)
    aux4_t, aux4_b = sb(pe, "aux4", [128, 4, 4], F32)
    auxf_t, auxf_b = sb(pe, "auxf", [128, 4], F32)
    scati_t, scati_b = sb(pe, "scati", [128, 1], I32)
    auxs_t, auxs_b = sb(pe, "auxs", [128, 17, 4], F32)
    toki_t, toki_b = sb(pe, "toki", [128, 17], I32)
    xcb_t, xcb_b = sb(pe, "xcb", [128, 2, D], BF16)
    xf_t, xf_b = sb(pe, "xf", [128, 8, CAP], BF16, is_transpose_dest=True)
    wg_t, wg_b = sb(pe, "wg_t", [128, 8, 2048], BF16)
    wu_t, wu_b = sb(pe, "wu_t", [128, 8, 2048], BF16)
    wd_t, wd_b = sb(pe, "wd_t", [128, 16, D], BF16)
    gg_t, gg_b = sb(pe, "gg", [128, 512], BF16)
    uu_t, uu_b = sb(pe, "uu", [128, 512], BF16)
    hbuf_t, hbuf_b = sb(pe, "hbuf", [128, 16, 512], BF16)
    eos_t, eos_b = sb(pe, "eos", [128, 2, D], BF16)
    eosb = [Buf("eos0"), Buf("eos1")]
    zz_t, zz_b = sb(pe, "zz", [128, 68], F32)

    k.op("vector", lambda v: v.memzero(zz_t[:]), writes=[zz_b])
    for q in range(4):
        k.dma("sync", auxc_dram[q].rearrange("(a p) w -> p a w", p=128),
              zz_t[:].rearrange("p (a w) -> p a w", w=4),
              reads=[zz_b], writes=[auxd_b], stream="xcw")

    def load_w(el):
        k.dma("gpsimd", wg_t[:], wgT_in[el].rearrange("(c p) f -> p c f", p=128),
              writes=[wg_b], stream="gw")
        k.dma("gpsimd", wu_t[:], wuT_in[el].rearrange("(c p) f -> p c f", p=128),
              writes=[wu_b], stream="gw")
        k.dma("gpsimd", wd_t[:], wdT_in[el].rearrange("(c p) d -> p c d", p=128),
              writes=[wd_b], stream="gw")

    load_w(0)
    OOB = 3000.0
    for el in range(4):
        k.mark(f"prep{el}")
        k.op("tensor", lambda t, el=el: t.transpose(ps[4][:, 0:128],
                                                    kth_t[:, el, :], ident_t[:]),
             reads=[kth_b, ident_b], writes=[psb[4]])
        k.op("vector", lambda v: v.tensor_copy(sc_std[:], ps[4][:, 0:128]),
             reads=[psb[4]], writes=[scstd_b])
        k.op("vector", lambda v, el=el: v.tensor_scalar(
                 mpf_t[:], sc_std[:], nxt4_t[:, el:el + 1], None, op0=ALU.is_gt),
             reads=[scstd_b, nxt4_b], writes=[mpf_b])
        k.op("vector", lambda v: v.tensor_copy(mpb_t[:], mpf_t[:]),
             reads=[mpf_b], writes=[mpb_b])
        k.op("tensor", lambda t: t.matmul(ps[4][:, 0:128], tril_t[:], mpb_t[:],
                                          start=True, stop=True),
             reads=[tril_b, mpb_b], writes=[psb[4]])
        k.op("tensor", lambda t: t.matmul(ps[5][0:1, 0:128], ones128b_t[:],
                                          mpb_t[:], start=True, stop=True),
             reads=[ones128b_b, mpb_b], writes=[psb[5]])
        k.op("vector", lambda v: v.tensor_copy(csr2_t[0:1, 0:128], ps[5][0:1, 0:128]),
             reads=[psb[5]], writes=[csr2_b])
        for sh in (1, 2, 4, 8, 16, 32, 64):
            k.op("vector", lambda v, sh=sh: v.tensor_add(
                     csr2_t[0:1, sh:128], csr2_t[0:1, sh:128], csr2_t[0:1, 0:128 - sh]),
                 reads=[csr2_b], writes=[csr2_b])
        k.op("vector", lambda v: v.tensor_sub(csr2_t[0:1, 128:256], csr2_t[0:1, 0:128],
                                              ps[5][0:1, 0:128]),
             reads=[csr2_b, psb[5]], writes=[csr2_b])
        k.op("tensor", lambda t: t.matmul(ps[6][:, 0:128], ones1_t[:],
                                          csr2_t[0:1, 128:256], start=True, stop=True),
             reads=[csr2_b, ones1_b], writes=[psb[6]])
        k.op("vector", lambda v: v.tensor_copy(p1sb_t[:], ps[4][:, 0:128]),
             reads=[psb[4]], writes=[p1sb_b])
        k.op("vector", lambda v: v.tensor_add(grank_t[:], p1sb_t[:], ps[6][:, 0:128]),
             reads=[p1sb_b, psb[6]], writes=[grank_b])
        k.op("vector", lambda v: v.tensor_scalar(scat_t[:], grank_t[:], -OOB,
                                                 None, op0=ALU.add),
             reads=[grank_b], writes=[scat_b])
        k.op("vector", lambda v: v.tensor_mul(scat_t[:], scat_t[:], mpf_t[:]),
             reads=[scat_b, mpf_b], writes=[scat_b])
        k.op("vector", lambda v: v.tensor_scalar(scat_t[:], scat_t[:], OOB,
                                                 None, op0=ALU.add),
             reads=[scat_b], writes=[scat_b])
        k.op("vector", lambda v: v.tensor_scalar(lsel_t[:], p1sb_t[:], 5.0, None,
                                                 op0=ALU.add),
             reads=[p1sb_b], writes=[lsel_b])
        k.op("vector", lambda v: v.tensor_mul(lsel_t[:], lsel_t[:], mpf_t[:]),
             reads=[lsel_b, mpf_b], writes=[lsel_b])
        k.op("vector", lambda v: v.tensor_scalar(lsel_t[:], lsel_t[:], -5.0, None,
                                                 op0=ALU.add),
             reads=[lsel_b], writes=[lsel_b])

        k.mark(f"pack{el}")
        for grp in range(32):
            k.op("vector", lambda v, grp=grp: v.tensor_tensor(
                     self_f[:], iota32r_t[:, None, :].broadcast_to([128, 4, 32]),
                     lsel_t[:, ts(grp, 4)][:, :, None].broadcast_to([128, 4, 32]),
                     op=ALU.is_equal),
                 reads=[iota32r_b, lsel_b], writes=[selff_b])
            k.op("vector", lambda v, grp=grp: v.tensor_copy(
                     aux4_t[:, :, 0], scat_t[:, ts(grp, 4)]),
                 reads=[scat_b], writes=[aux4_b])
            k.op("vector", lambda v, grp=grp: v.tensor_copy(
                     aux4_t[:, :, 1], tloc_t[:, ts(grp, 4)]),
                 reads=[tloc_b], writes=[aux4_b])
            k.op("vector", lambda v, grp=grp: v.tensor_copy(
                     aux4_t[:, :, 2], sc_std[:, ts(grp, 4)]),
                 reads=[scstd_b], writes=[aux4_b])
            k.op("vector", lambda v, grp=grp: v.tensor_copy(
                     aux4_t[:, :, 3], mpf_t[:, ts(grp, 4)]),
                 reads=[mpf_b], writes=[aux4_b])
            k.op("tensor", lambda t: t.matmul(
                     ps[2][:, 0:16], self_f[:].rearrange("p a b -> p (a b)"),
                     aux4_t[:].rearrange("p a b -> p (a b)"), start=True, stop=True),
                 reads=[selff_b, aux4_b], writes=[psb[2]])
            for i in range(4):
                k.op("vector", lambda v, i=i: v.tensor_copy(
                         auxf_t[32 * i:32 * i + 32, :],
                         ps[2][32 * i:32 * i + 32, 4 * i:4 * i + 4]),
                     reads=[psb[2]], writes=[auxf_b])
            k.op("vector", lambda v: v.scalar_tensor_tensor(
                     auxf_t[:, 0:1], auxf_t[:, 3:4], -OOB, auxf_t[:, 0:1],
                     op0=ALU.mult, op1=ALU.add),
                 reads=[auxf_b], writes=[auxf_b])
            k.op("vector", lambda v: v.tensor_scalar(auxf_t[:, 0:1], auxf_t[:, 0:1],
                                                     OOB, None, op0=ALU.add),
                 reads=[auxf_b], writes=[auxf_b])
            k.op("vector", lambda v: v.tensor_copy(scati_t[:], auxf_t[:, 0:1]),
                 reads=[auxf_b], writes=[scati_b])
            k.dop("gpsimd", lambda g, el=el: g.indirect_dma_start(
                      auxc_dram[el], IOA(scati_t[:], 0), auxf_t[:], None,
                      bounds_check=breg(g, RB, CAP - 1), oob_is_err=False),
                  reads=[scati_b, auxf_b], writes=[auxd_b], stream="xcw")

        k.mark(f"gath{el}")
        k.dma("sync", auxs_t[:], auxc_dram[el].rearrange("(a p) w -> p a w", p=128),
              reads=[auxd_b], writes=[auxs_b], stream="xcr")
        k.op("vector", lambda v: v.tensor_copy(toki_t[:], auxs_t[:, :, 1]),
             reads=[auxs_b], writes=[toki_b])
        for blk in range(16):
            k.dop("gpsimd", lambda g, blk=blk: g.indirect_dma_start(
                      xcb_t[:, blk % 2, :], None,
                      ag2h_out.rearrange("c t d -> (c t) d"),
                      IOA(toki_t[:, blk:blk + 1], 0),
                      bounds_check=breg(g, RB, 8 * T - 1), oob_is_err=False),
                  reads=[toki_b, ag2ho_b], writes=[xcb_b], stream="gg")
            k.dma("sync", xf_t[:, :, ts(blk, 128)],
                  xcb_t[:, blk % 2, :].rearrange("p (c n) -> p c n", n=128),
                  reads=[xcb_b], writes=[xf_b], stream="tr", transpose=True)

        k.mark(f"ffn{el}")
        for sup in range(4):
            for j in range(16):
                pj = j % 2

                def mm_gx(t, sup=sup, j=j, pj=pj):
                    last = None
                    for kk in range(8):
                        last = t.matmul(ps[pj][:, :], wg_t[:, kk, ts(j, 128)],
                                        xf_t[:, kk, ts(sup, 512)],
                                        start=(kk == 0), stop=(kk == 7))
                    return last

                k.op("tensor", mm_gx, reads=[wg_b, xf_b], writes=[psb[pj]])
                k.op("scalar", lambda s, pj=pj: s.activation(gg_t[:], ps[pj][:],
                                                             AF.Gelu),
                     reads=[psb[pj]], writes=[gg_b])

                def mm_ux(t, sup=sup, j=j, pj=pj):
                    last = None
                    for kk in range(8):
                        last = t.matmul(ps[2 + pj][:, :], wu_t[:, kk, ts(j, 128)],
                                        xf_t[:, kk, ts(sup, 512)],
                                        start=(kk == 0), stop=(kk == 7))
                    return last

                k.op("tensor", mm_ux, reads=[wu_b, xf_b], writes=[psb[2 + pj]])
                k.op("scalar", lambda s, pj=pj: s.copy(uu_t[:], ps[2 + pj][:]),
                     reads=[psb[2 + pj]], writes=[uu_b])
                k.op("vector", lambda v, j=j: v.tensor_mul(hbuf_t[:, j, :], gg_t[:],
                                                           uu_t[:]),
                     reads=[gg_b, uu_b], writes=[hbuf_b])
            if sup == 3 and el + 1 < 4:
                k.dma("gpsimd", wg_t[:],
                      wgT_in[el + 1].rearrange("(c p) f -> p c f", p=128),
                      writes=[wg_b], stream="gw")
                k.dma("gpsimd", wu_t[:],
                      wuT_in[el + 1].rearrange("(c p) f -> p c f", p=128),
                      writes=[wu_b], stream="gw")
            for sbk in range(4):
                for half in range(2):
                    bank = 4 + half

                    def mm_dn(t, sbk=sbk, half=half, bank=bank):
                        last = None
                        for j in range(16):
                            last = t.matmul(ps[bank][:, :],
                                            hbuf_t[:, j, ts(sbk, 128)],
                                            wd_t[:, j, ts(half, 512)],
                                            start=(j == 0), stop=(j == 15))
                        return last

                    k.op("tensor", mm_dn, reads=[hbuf_b, wd_b], writes=[psb[bank]])
                    k.op("vector", lambda v, sup=sup, sbk=sbk, half=half, bank=bank:
                         v.tensor_scalar(eos_t[:, (sup * 4 + sbk) % 2, ts(half, 512)],
                                         ps[bank][:],
                                         auxs_t[:, sup * 4 + sbk, 2:3], None,
                                         op0=ALU.mult),
                         reads=[psb[bank], auxs_b],
                         writes=[eosb[(sup * 4 + sbk) % 2]])
                blk = sup * 4 + sbk
                k.dop("gpsimd", lambda g, blk=blk: g.indirect_dma_start(
                          p_dram, IOA(toki_t[:, blk:blk + 1], 0),
                          eos_t[:, blk % 2, :], None,
                          bounds_check=breg(g, RB, 8 * T - 1), oob_is_err=False,
                          compute_op=ALU.add),
                      reads=[eosb[blk % 2], toki_b], writes=[pd_b], stream="gg")
        if el + 1 < 4:
            k.dma("gpsimd", wd_t[:],
                  wdT_in[el + 1].rearrange("(c p) d -> p c d", p=128),
                  writes=[wd_b], stream="gw")

    if debug:
        dbg_aux = dout("dbg_aux", [128, 17, 4], F32)
        final.append(k.dma("gpsimd", dbg_aux, auxs_t[:], reads=[auxs_b], stream="dbg"))
    pe.close()
    k.fence()


def emit_final(nc, k, st, final, debug, L, EX):
    k.mark("final")
    sb = L["sb"]
    ps, psb = L["ps"], L["psb"]
    p3 = L["phase3_out"]
    x2_dram, x2d_b = p3["x2_dram"], p3["x2d_b"]
    sh_dram, shd_b = EX["sh_dram"], EX["shd_b"]
    p_dram, pd_b = EX["p_dram"], EX["pd_b"]
    rs_out, rso_b = EX["rs_out"], EX["rso_b"]
    out_final, outf_b = EX["out_final"], EX["outf_b"]
    G4 = EX["G4"]

    pe = ExitStack()
    W = 4
    x2r_t, x2r_b = sb(pe, "x2r", [128, W, D], F32)
    shr_t, shr_b = sb(pe, "shr", [128, W, D], F32)
    rsr_t, rsr_b = sb(pe, "rsr", [128, W, D], BF16)
    ot, ot_b = [], []
    for i in range(W):
        t, b = sb(pe, f"otile{i}", [128, D], F32)
        ot.append(t)
        ot_b.append(b)

    for tb in range(32):
        sl = tb % W
        k.dma("sync", x2r_t[:, sl, :],
              x2_dram.rearrange("(s p) d -> s p d", p=128)[tb],
              reads=[x2d_b], writes=[x2r_b], stream="x2l")
        k.dma("sync", shr_t[:, sl, :],
              sh_dram.rearrange("(s p) d -> s p d", p=128)[tb],
              reads=[shd_b], writes=[shr_b], stream="shl")
        k.dma("sync", rsr_t[:, sl, :],
              rs_out.rearrange("(s p) d -> s p d", p=128)[tb],
              reads=[rso_b], writes=[rsr_b], stream="rsl")
        k.op("vector", lambda v, sl=sl: v.tensor_add(
                 ot[sl][:], x2r_t[:, sl, :], shr_t[:, sl, :]),
             reads=[x2r_b, shr_b], writes=[ot_b[sl]])
        k.op("vector", lambda v, sl=sl: v.tensor_add(
                 ot[sl][:], ot[sl][:], rsr_t[:, sl, :]),
             reads=[rsr_b, ot_b[sl]], writes=[ot_b[sl]])
        final.append(k.dma("sync",
                           out_final.rearrange("(s p) d -> s p d", p=128)[tb],
                           ot[sl][:], reads=[ot_b[sl]], writes=[outf_b],
                           stream="out"))
    if L.get("probe"):
        dout = L["dout"]
        pr_x2 = dout("pr_x2", [T, D], F32)
        pr_sh = dout("pr_sh", [T, D], F32)
        pr_h2 = dout("pr_h2", [T, D], BF16)
        pr_rs = dout("pr_rs", [256, D], F32)
        final.append(k.dma("gpsimd", pr_x2, x2_dram, reads=[x2d_b], stream="dbg"))
        final.append(k.dma("gpsimd", pr_sh, sh_dram, reads=[shd_b], stream="dbg"))
        final.append(k.dma("gpsimd", pr_h2, p3["h2_dram"], reads=[p3["h2d_b"]], stream="dbg"))
        final.append(k.dma("gpsimd", pr_rs, rs_out[0:256, :], reads=[rso_b], stream="dbg"))
    pe.close()



def _consts():
    epse = np.zeros((128, 512), np.float32)
    for p in range(128):
        for lh in range(8):
            epse[p, lh * 64 + (p % 64)] = 1e-4
    gjm = np.ones((128, 16, 16), np.float32)
    for j in range(16):
        gjm[:, j, j] = 0
    gjm8 = np.ones((128, 8, 8), np.float32)
    for j in range(8):
        gjm8[:, j, j] = 0
    idc = np.zeros((128, 16), np.float32)
    for p in range(128):
        if p % 64 < 16:
            idc[p, p % 64] = 1.0
    BF = ml_dtypes.bfloat16
    ident = np.eye(128, dtype=np.float32)
    iota32r = np.broadcast_to(np.arange(32, dtype=np.float32), (128, 32)).copy()
    tril = np.triu(np.ones((128, 128), np.float32), 1).astype(BF)
    ones1 = np.ones((1, 128), np.float32)
    ones128 = np.ones((128, 1), np.float32)
    ones128b = np.ones((128, 1), BF)
    return dict(epse=epse, gjm=gjm, gjm8=gjm8, idc=idc, ident=ident,
                iota32r=iota32r, tril=tril, ones1=ones1, ones128=ones128,
                ones128b=ones128b)


def _core_consts(c):
    g, r = c // 4, c % 4
    kth4 = np.zeros((128, 4), np.int32)
    for pl in range(4):
        e = 4 * r + pl
        for w in range(128):
            cg, s = w // 32, w % 32
            kth4[w, pl] = ((4 * g + cg) * 16 + e) * 32 + s
    tloc = np.zeros((128, 128), np.float32)
    for w in range(128):
        for p in range(128):
            tloc[p, w] = (4 * g + w // 32) * 4096 + (w % 32) * 128 + p
    cf = np.zeros(8, np.float32)
    for j in range(4):
        if j < r:
            cf[g * 4 + j] = ALPHA ** (32 * (r - 1 - j))
    coef = np.ascontiguousarray(np.broadcast_to(cf, (128, 8)), dtype=np.float32)
    return dict(kth4=kth4, tloc=tloc, coef=coef)


def kernel(**inputs):
    X = np.ascontiguousarray(inputs["L0_attn_x"], dtype=np.float32).reshape(-1, D)
    Wq = np.ascontiguousarray(inputs["L0_attn_Wq"], dtype=np.float32)
    Wk = np.ascontiguousarray(inputs["L0_attn_Wk"], dtype=np.float32)
    Wv = np.ascontiguousarray(inputs["L0_attn_Wv"], dtype=np.float32)
    Wo = np.ascontiguousarray(inputs["L0_attn_Wo"], dtype=np.float32)
    lng = np.ascontiguousarray(
        np.broadcast_to(inputs["L0_attn_ln_g"], (128, D)), dtype=np.float32)
    lnb = np.ascontiguousarray(
        np.broadcast_to(inputs["L0_attn_ln_b"], (128, D)), dtype=np.float32)
    gwT = np.ascontiguousarray(inputs["L0_ffn_gate_w"].T, dtype=np.float32)
    wsgT = np.ascontiguousarray(inputs["L0_ffn_Ws_g"].T, dtype=np.float32)
    wsuT = np.ascontiguousarray(inputs["L0_ffn_Ws_u"].T, dtype=np.float32)
    wsdT = np.ascontiguousarray(inputs["L0_ffn_Ws_d"].T, dtype=np.float32)
    Wg, Wu, Wd = inputs["L0_ffn_Wg"], inputs["L0_ffn_Wu"], inputs["L0_ffn_Wd"]

    nc = build(phase=4, debug=False, ffn_stop=99)
    shared = _consts()
    in_maps = []
    for c in range(8):
        r = c % 4
        im = {"x": np.ascontiguousarray(X[c * T:(c + 1) * T]),
              "wq": Wq, "wk": Wk, "wv": Wv, "wo": Wo, "lng": lng, "lnb": lnb,
              "gwT": gwT, "wsgT": wsgT, "wsuT": wsuT, "wsdT": wsdT}
        im.update(shared)
        im.update(_core_consts(c))
        for pl in range(4):
            e = 4 * r + pl
            im[f"wgT{pl}"] = np.ascontiguousarray(Wg[e].T, dtype=np.float32)
            im[f"wuT{pl}"] = np.ascontiguousarray(Wu[e].T, dtype=np.float32)
            im[f"wdT{pl}"] = np.ascontiguousarray(Wd[e].T, dtype=np.float32)
        in_maps.append(im)
    res = run_bass_kernel_spmd(nc, in_maps, core_ids=list(range(8)))
    out = np.concatenate([np.asarray(res.results[c]["out"], dtype=np.float32)
                          for c in range(8)], axis=0)
    return out.reshape(2, 16384, D)
